# version 03 (final)
# speedup vs baseline: 1.0872x; 1.0274x over previous
import contextlib
import numpy as np
import concourse.bass as bass
import concourse.mybir as mybir
from concourse.bass_types import AP
from concourse.bass_utils import run_bass_kernel_spmd
from concourse.aot_env import allow_ici_all_to_all

F32 = mybir.dt.float32
BF16 = mybir.dt.bfloat16
I32 = mybir.dt.int32
AF = mybir.ActivationFunctionType
ALU = mybir.AluOpType
AX = mybir.AxisListType

NCORE = 8
TSL = 2048
HALO = 128
TT = TSL + HALO
DM = 1024
DXZ = 4096
DI = 2048
DSL = DI // NCORE
NST = 16
DTR = 64
DFF = 4096
NMT = TSL // 128
EPS0 = 1e-5
EPSA = 1e-12

R_QB, R_KB, R_WG, R_PB, R_FB, R_D2B, R_FOB, R_LN0G, R_LN0B, R_LNIG, \
    R_LNIB, R_LN1G, R_LN1B, R_LN2G, R_LN2B, R_ZERO = range(16)

DBG_TAPS = []
PHASE_LIMIT = 99
SUB10 = 3


class Sem:
    def __init__(self, es, nc, name):
        self.h = es.enter_context(nc.semaphore(name))
        self.c = 0

    def inc(self, instr, n=1):
        instr.then_inc(self.h, n)
        self.c += n
        return instr



def chunk_cols(t, C, X, off=0):
    return AP(t.ap().tensor, off, [[X, 128], [128 * X, C], [1, X]])

def brow(hbm_ap, p=128):
    n = hbm_ap.shape[-1]
    return AP(hbm_ap.tensor, hbm_ap.offset, [[0, p], [1, n]])


class KB:

    def __init__(self):
        nc = self.nc = bass.Bass("TRN2", target_bir_lowering=False)
        es = self.es = contextlib.ExitStack()

        din = lambda n, s, d=F32: nc.dram_tensor(n, s, d, kind="ExternalInput")
        dint = lambda n, s, d=F32: nc.dram_tensor(n, s, d, kind="Internal")
        self.x_in = din("x", [TT, DM])
        self.w_ip_T = din("w_ip_T", [DM, DXZ])
        self.w_xp_T = din("w_xp_T", [DI, 96])
        self.w_dt_T = din("w_dt_T", [DTR, DI])
        self.conv_wc = din("conv_wc", [16, 128, 4])
        self.conv_bc = din("conv_bc", [16, 128, 1])
        self.dt_bc = din("dt_bc", [16, 128, 1])
        self.a_log = din("a_log", [2, 128, NST])
        self.d_skip = din("d_skip", [16, 128, 1])
        self.w_op_T = din("w_op_T", [DI, DM])
        self.w_q_T = din("w_q_T", [DM, DM])
        self.w_k_T = din("w_k_T", [DM, DM])
        self.w_p_T = din("w_p_T", [DM, DM])
        self.w_f_T = din("w_f_T", [DM, DM])
        self.w_d1_T = din("w_d1_T", [DM, DFF])
        self.w_d2_T = din("w_d2_T", [DFF, DM])
        self.d1_bc = din("d1_bc", [32, 128, 1])
        self.w_fi = din("w_fi", [DM, DFF])
        self.w_fi_T = din("w_fi_T", [DFF, DM])
        self.fi_bc = din("fi_bc", [32, 128, 1])
        self.w_fo = din("w_fo", [DFF, DM])
        self.rows_in = din("rows_in", [16, DM])
        self.onehot_r = din("onehot_r", [1, 4])
        self.onehot_c = din("onehot_c", [4, 1])
        self.ident = din("ident", [128, 128])
        self.out_t = nc.dram_tensor("out", [TSL, DM], F32, kind="ExternalOutput")

        self.xnT_hbm = dint("xnT_hbm", [8, 128, TT])
        self.u_hbm = dint("u_hbm", [16, 128, TSL])
        self.sres_hbm = dint("sres_hbm", [16, 128, TSL])
        self.uD_hbm = dint("uD_hbm", [16, 128, TSL])
        self.dd_in = dint("dd_in", [NCORE, 2, DSL, TSL])
        self.dd_out = dint("dd_out", [NCORE, 2, DSL, TSL])
        self.bc_in = dint("bc_in", [32, TSL])
        self.bc_out = dint("bc_out", [NCORE, 32, TSL])
        self.y_in = dint("y_in", [NCORE, DSL, TSL])
        self.y_out = dint("y_out", [NCORE, DSL, TSL])
        self.amx_in = dint("amx_in", [1, 8])
        self.amx_out = dint("amx_out", [1, 8])
        self.hmx_in = dint("hmx_in", [1, 8])
        self.hmx_out = dint("hmx_out", [1, 8])
        self.gar_in = dint("gar_in", [4, DM + 8])
        self.gar_out = dint("gar_out", [4, DM + 8])
        self.csrow_hbm = dint("csrow_hbm", [1, DFF])
        self.hid_hbm = dint("hid_hbm", [32, 128, TSL])
        self.wqi_hbm = dint("wqi_hbm", [DM, DFF], BF16)
        self.wqo_hbm = dint("wqo_hbm", [DFF, DM], BF16)
        self.wos_hbm = dint("wos_hbm", [32, 128, 1])
        self.g_hbm = dint("g_hbm", [16, 128, DM])
        self.h_hbm = dint("h_hbm", [16, 128, DM])
        self.h2_hbm = dint("h2_hbm", [16, 128, DM])
        self.wsp = {}
        for nm, shape in [("wop", [DI, DM]), ("wq", [DM, DM]), ("wk", [DM, DM]),
                          ("wp", [DM, DM]), ("wf", [DM, DM]),
                          ("wd1", [DM, DFF]), ("wd2", [DFF, DM])]:
            self.wsp[nm] = (dint(nm + "_h", shape, BF16), dint(nm + "_l", shape, BF16))
        self.hT_sp = (dint("hT_h", [8, 128, TSL], BF16), dint("hT_l", [8, 128, TSL], BF16))
        self.gkT_sp = (dint("gkT_h", [8, 128, TSL], BF16), dint("gkT_l", [8, 128, TSL], BF16))
        self.hlT_sp = (dint("hlT_h", [8, 128, TSL], BF16), dint("hlT_l", [8, 128, TSL], BF16))
        self.h2T_sp = (dint("h2T_h", [8, 128, TSL], BF16), dint("h2T_l", [8, 128, TSL], BF16))
        self.yfT_sp = (dint("yfT_h", [16, 128, TSL], BF16), dint("yfT_l", [16, 128, TSL], BF16))
        self.k_hbm = dint("k_hbm", [16, 128, DM])
        self.hqs_h = dint("hqs_h", [32, 128, TSL], BF16)
        self.hqs_l = dint("hqs_l", [32, 128, TSL], BF16)

        self.dbg = {}
        for name in DBG_TAPS:
            shp, dt = {"xnT": ([8, 128, TT], F32), "u": ([16, 128, TSL], F32),
                   "dd": ([NCORE, 2, DSL, TSL], F32), "bc": ([NCORE, 32, TSL], F32),
                   "y": ([NCORE, DSL, TSL], F32), "m": ([16, 128, DM], F32),
                   "hid": ([32, 128, TSL], F32), "g": ([16, 128, DM], F32),
                   "h": ([16, 128, DM], F32), "s": ([16, 128, DM], F32),
                   "xdbl": ([128, TSL], F32), "amax": ([1, 8], F32),
                   "p0stats": ([128, 512], F32), "p0xn": ([128, 4096], F32),
                   "p0ev": ([128, 4096], F32),
                   "wqi": ([DM, DFF], BF16), "wqo": ([DFF, DM], BF16),
                   "wd1h": ([DM, DFF], BF16), "wd1l": ([DM, DFF], BF16),
                   "wos": ([128, 32], F32), "xq": ([128, 16384], BF16),
                   "yf": ([16, 128, TSL], F32),
                   "k": ([16, 128, DM], F32), "G": ([1, DM], F32),
                   "h2": ([16, 128, DM], F32),
                   "gkT": ([8, 128, TSL], BF16), "gkTl": ([8, 128, TSL], BF16),
                   "scol": ([128, 256], F32)}[name]
            self.dbg[name] = nc.dram_tensor("dbgo_" + name, shp, dt,
                                            kind="ExternalOutput")

        sbt = lambda n, s, d=F32: es.enter_context(nc.sbuf_tensor(n, s, d))
        self.xnT_hs = sbt("xnT_hs", [128, 8 * TT], BF16)
        self.xnT_ls = sbt("xnT_ls", [128, 8 * TT], BF16)
        self.ident_s = sbt("ident_s", [128, 128])
        self.consts = sbt("consts", [128, 256])
        self.rows_s = sbt("rows_s", [16, DM])
        self.oh_r = sbt("oh_r", [1, 4])
        self.oh_c = sbt("oh_c", [4, 1])
        self.fa = sbt("fa", [128, 19450])
        self.ba = sbt("ba", [128, 18000], BF16)
        self.ia = sbt("ia", [128, 4096], I32)
        self.ps_a = es.enter_context(nc.psum_tensor("ps_a", [128, 2048], F32))
        self.ps_b = es.enter_context(nc.psum_tensor("ps_b", [128, 2048], F32))

        self.S = {n: Sem(es, nc, n) for n in
                  ["ld", "ld2", "st", "te", "vec", "sc", "gp", "cc", "fv", "fs",
                   "ld3", "st3"]}
        self.blk = es.enter_context(nc.Block())

    def snap(self):
        return {k: s.c for k, s in self.S.items()}

    def row(self, r):
        return self.rows_s[r:r + 1, :]

    def split3(self, e, src, dst_h, dst_l, tmp):
        e.tensor_copy(out=dst_h, in_=src)
        e.tensor_copy(out=tmp, in_=dst_h)
        e.tensor_sub(out=tmp, in0=src, in1=tmp)
        e.tensor_copy(out=dst_l, in_=tmp)

    CO_CW, CO_CB, CO_DTB, CO_DSK, CO_A, CO_D1B, CO_FIB, CO_WOS, CO_SC = \
        0, 64, 80, 96, 112, 144, 176, 208, 240
    CC_ONE, CC_NEG1, CC_EPS0, CC_EPSA, CC_EPS8 = 240, 241, 242, 243, 244

    def ph_setup(self):
        K = self
        S = K.S
        ld, sc, vec = S["ld"], S["sc"], S["vec"]

        def f_sync(e):
            ld.inc(e.dma_start(out=K.ident_s[:], in_=K.ident.ap()), 16)
            ld.inc(e.dma_start(out=K.consts[:, K.CO_CW:K.CO_CW + 64],
                        in_=chunk_cols(K.conv_wc, 16, 4)), 16)
            ld.inc(e.dma_start(out=K.consts[:, K.CO_CB:K.CO_CB + 16],
                        in_=chunk_cols(K.conv_bc, 16, 1),
                        allow_slow_non_contiguous=True), 16)
            ld.inc(e.dma_start(out=K.consts[:, K.CO_DTB:K.CO_DTB + 16],
                        in_=chunk_cols(K.dt_bc, 16, 1),
                        allow_slow_non_contiguous=True), 16)
            ld.inc(e.dma_start(out=K.consts[:, K.CO_DSK:K.CO_DSK + 16],
                        in_=chunk_cols(K.d_skip, 16, 1),
                        allow_slow_non_contiguous=True), 16)
            ld.inc(e.dma_start(out=K.consts[:, K.CO_A:K.CO_A + 32],
                        in_=chunk_cols(K.a_log, 2, 16)), 16)
            ld.inc(e.dma_start(out=K.consts[:, K.CO_D1B:K.CO_D1B + 32],
                        in_=chunk_cols(K.d1_bc, 32, 1),
                        allow_slow_non_contiguous=True), 16)
            ld.inc(e.dma_start(out=K.consts[:, K.CO_FIB:K.CO_FIB + 32],
                        in_=chunk_cols(K.fi_bc, 32, 1),
                        allow_slow_non_contiguous=True), 16)
            ld.inc(e.dma_start(out=K.rows_s[:, :], in_=K.rows_in.ap()), 16)
            ld.inc(e.dma_start(out=K.oh_r[:1, :], in_=K.onehot_r.ap()), 16)
            ld.inc(e.dma_start(out=K.oh_c[:, :], in_=K.onehot_c.ap()), 16)

        B = None

        def f_scalar(e):
            e.wait_ge(ld.h, B["ld"] + 16 * 11)
            sc.inc(e.activation(out=K.consts[:, K.CO_A:K.CO_A + 32],
                                in_=K.consts[:, K.CO_A:K.CO_A + 32], func=AF.Exp), 1)

        def f_vector(e):
            e.memset(K.consts[:, K.CC_ONE:K.CC_ONE + 1], 1.0)
            e.memset(K.consts[:, K.CC_NEG1:K.CC_NEG1 + 1], -1.0)
            e.memset(K.consts[:, K.CC_EPS0:K.CC_EPS0 + 1], EPS0)
            e.memset(K.consts[:, K.CC_EPSA:K.CC_EPSA + 1], EPSA)
            e.memset(K.consts[:, K.CC_EPS8:K.CC_EPS8 + 1], 1e-8)
            e.wait_ge(sc.h, B["sc"] + 1)
            vec.inc(e.tensor_scalar(out=K.consts[:, K.CO_A:K.CO_A + 32],
                                    in0=K.consts[:, K.CO_A:K.CO_A + 32],
                                    scalar1=-1.0, scalar2=None, op0=ALU.mult), 1)

        B = self.snap()
        K.blk.sync(f_sync)
        K.blk.scalar(f_scalar)
        K.blk.vector(f_vector)

    def ph_p0(self):
        K = self
        S = K.S
        ld, te, vec, sc, gp, cc, fv = (S[n] for n in
                                       ["ld", "te", "vec", "sc", "gp", "cc", "fv"])
        FX, FST = 0, 17440

        B = self.snap()

        def f_sync(e):
            for h0, hn in ((0, 9), (9, 8)):
                ld.inc(e.dma_start(
                    out=K.fa[:, FX + h0 * 1024:FX + (h0 + hn) * 1024].rearrange(
                        "p (mt r) -> p mt r", mt=hn),
                    in_=chunk_cols(K.x_in, hn, 1024, off=h0 * 128 * 1024)), 16)

        def vfence(e):
            fvt = fv.c
            fv.inc(e.engine_nop(), 1)
            e.wait_ge(fv.h, fvt + 1)

        def f_vector(e):
            e.wait_ge(ld.h, B["ld"] + 32)
            for mt in range(17):
                e.tensor_reduce(out=K.fa[:, FST + mt:FST + mt + 1],
                                in_=K.fa[:, FX + mt * 1024:FX + (mt + 1) * 1024],
                                axis=AX.X, op=ALU.add)
            vec.inc(e.engine_nop(), 1)
            e.wait_ge(sc.h, B["sc"] + 1)
            m = K.fa[:, FST:FST + 17]
            ssc = K.fa[:, FST + 96:FST + 113]
            var = K.fa[:, FST + 64:FST + 81]
            vfence(e)
            e.tensor_scalar(out=m, in0=m, scalar1=1.0 / DM, scalar2=None, op0=ALU.mult)
            e.tensor_scalar(out=ssc, in0=ssc, scalar1=1.0 / DM, scalar2=None,
                            op0=ALU.mult)
            vfence(e)
            e.tensor_mul(out=var, in0=m, in1=m)
            vfence(e)
            e.tensor_sub(out=var, in0=ssc, in1=var)
            vec.inc(e.engine_nop(), 1)
            e.wait_ge(sc.h, B["sc"] + 2)
            r = K.fa[:, FST + 448:FST + 465]
            mr = K.fa[:, FST + 128:FST + 145]
            e.reciprocal(out=r, in_=K.fa[:, FST + 32:FST + 49])
            vfence(e)
            e.tensor_mul(out=mr, in0=m, in1=r)
            vfence(e)
            for mt in range(17):
                e.tensor_scalar(out=K.fa[:, FX + mt * 1024:FX + (mt + 1) * 1024],
                                in0=K.fa[:, FX + mt * 1024:FX + (mt + 1) * 1024],
                                scalar1=r[:, mt:mt + 1], scalar2=mr[:, mt:mt + 1],
                                op0=ALU.mult, op1=ALU.subtract)
            for mt in range(17):
                e.tensor_reduce(out=K.fa[:, FST + 176 + mt:FST + 177 + mt],
                                in_=K.fa[:, FX + mt * 1024:FX + (mt + 1) * 1024],
                                axis=AX.X, op=ALU.max, apply_absolute_value=True)
            vfence(e)
            e.tensor_reduce(out=K.fa[:, FST + 160:FST + 161],
                            in_=K.fa[:, FST + 176:FST + 193],
                            axis=AX.X, op=ALU.max)
            vec.inc(e.engine_nop(), 1)
            for mt in range(17):
                e.wait_ge(te.h, B["te"] + 8 * (mt + 1) + 1)
                ev = FX + mt * 1024
                ps = K.ps_a if mt % 2 == 0 else K.ps_b
                e.tensor_copy(out=K.fa[:, ev:ev + 1024], in_=ps[:, 0:1024])
                vec.inc(e.engine_nop(), 1)
                seg = K.fa[:, ev:ev + 1024]
                hseg = K.xnT_hs[:, 0 * TT + mt * 128:0 * TT + (mt + 1) * 128]
                for kc in range(8):
                    e.tensor_copy(
                        out=K.xnT_hs[:, kc * TT + mt * 128:kc * TT + (mt + 1) * 128],
                        in_=K.fa[:, ev + kc * 128:ev + (kc + 1) * 128])
                tmpw = K.fa[:, 18420:19444]
                for kc in range(8):
                    e.tensor_copy(
                        out=tmpw[:, kc * 128:(kc + 1) * 128],
                        in_=K.xnT_hs[:, kc * TT + mt * 128:kc * TT + (mt + 1) * 128])
                for kc in range(8):
                    e.tensor_sub(out=tmpw[:, kc * 128:(kc + 1) * 128],
                                 in0=K.fa[:, ev + kc * 128:ev + (kc + 1) * 128],
                                 in1=tmpw[:, kc * 128:(kc + 1) * 128])
                for kc in range(8):
                    e.tensor_copy(
                        out=K.xnT_ls[:, kc * TT + mt * 128:kc * TT + (mt + 1) * 128],
                        in_=tmpw[:, kc * 128:(kc + 1) * 128])
                vec.inc(e.engine_nop(), 1)

        def f_scalar(e):
            e.wait_ge(ld.h, B["ld"] + 32)
            scr = K.fa[:, FST + 512:FST + 1536]
            for mt in range(17):
                ins = e.activation(out=scr,
                                   in_=K.fa[:, FX + mt * 1024:FX + (mt + 1) * 1024],
                                   func=AF.Square,
                                   accum_out=K.fa[:, FST + 96 + mt:FST + 97 + mt])
            sc.inc(ins, 1)
            e.wait_ge(vec.h, B["vec"] + 2)
            sc.inc(e.activation(out=K.fa[:, FST + 32:FST + 49],
                                in_=K.fa[:, FST + 64:FST + 81],
                                func=AF.Sqrt, bias=K.consts[:, K.CC_EPS0:K.CC_EPS0 + 1]), 1)

        def f_tensor(e):
            e.wait_ge(vec.h, B["vec"] + 3)
            te.inc(e.nop(), 1)
            for mt in range(17):
                if mt >= 2:
                    e.wait_ge(vec.h, B["vec"] + 3 + 2 * (mt - 2) + 1)
                ps = K.ps_a if mt % 2 == 0 else K.ps_b
                for kc in range(8):
                    te.inc(e.transpose(
                        out=ps[:, kc * 128:(kc + 1) * 128],
                        in_=K.fa[:, FX + mt * 1024 + kc * 128:
                                 FX + mt * 1024 + (kc + 1) * 128],
                        identity=K.ident_s[:]), 1)

        def f_gpsimd(e):
            e.wait_ge(vec.h, B["vec"] + 3)
            gpt = gp.c
            gp.inc(e.dma_start(out=K.amx_in.ap()[0:1, 2:3],
                               in_=K.fa[0:1, FST + 160:FST + 161]), 16)
            gp.inc(e.dma_start(
                out=AP(K.hid_hbm.ap().tensor, 0, [[1, 128]]),
                in_=K.fa[:, FST + 160:FST + 161]), 16)
            e.wait_ge(gp.h, gpt + 32)
            gp.inc(e.dma_start(
                out=K.fa[0:1, FST + 176:FST + 304],
                in_=AP(K.hid_hbm.ap().tensor, 0, [[128, 1], [1, 128]])), 16)
            e.wait_ge(vec.h, B["vec"] + 3 + 34)
            fa3 = K.fa[:, 0:17 * 1024].rearrange("p (mt r) -> p mt r", mt=17)
            for kc in range(8):
                src = fa3[:, :, kc * 128:(kc + 1) * 128]
                dst = K.xnT_hbm.ap()[kc, :, :].rearrange(
                    "p (mt t) -> p mt t", mt=17)
                gp.inc(e.dma_start(out=dst, in_=src), 16)

        def f_vector2(e):
            e.wait_ge(gp.h, B["gp"] + 48)
            vec.inc(e.tensor_reduce(out=K.fa[0:1, FST + 168:FST + 169],
                                    in_=K.fa[0:1, FST + 176:FST + 304],
                                    axis=AX.X, op=ALU.max), 1)

        def f_gpsimd2(e):
            e.wait_ge(vec.h, B["vec"] + 3 + 34 + 1)
            gpt = gp.c
            gp.inc(e.dma_start(out=K.amx_in.ap()[0:1, 0:1],
                               in_=K.fa[0:1, FST + 168:FST + 169]), 16)
            e.wait_ge(gp.h, gpt + 16)
            cc.inc(e.collective_compute(
                "AllReduce", ALU.max, replica_groups=[list(range(NCORE))],
                ins=[K.amx_in.ap().opt()], outs=[K.amx_out.ap().opt()]), 1)

        K.blk.sync(f_sync)
        K.blk.vector(f_vector)
        K.blk.scalar(f_scalar)
        K.blk.tensor(f_tensor)
        K.blk.gpsimd(f_gpsimd)
        K.blk.vector(f_vector2)
        K.blk.gpsimd(f_gpsimd2)

        if "p0xn" in K.dbg:
            Bx = self.snap()

            def f_dbgx(e):
                e.wait_ge(S["vec"].h, B["vec"] + 3)
                S["st"].inc(e.dma_start(out=K.dbg["p0xn"].ap()[:, :],
                                        in_=K.fa[:, 13 * 1024:17 * 1024]), 16)
                e.wait_ge(S["st"].h, S["st"].c)
            K.blk.sync(f_dbgx)

        if "p0ev" in K.dbg:
            Bx = self.snap()

            def f_dbge(e):
                e.wait_ge(S["vec"].h, Bx["vec"])
                S["st"].inc(e.dma_start(out=K.dbg["p0ev"].ap()[:, :],
                                        in_=K.fa[:, 0:4096]), 16)
                e.wait_ge(S["st"].h, S["st"].c)
            K.blk.sync(f_dbge)

        if "p0stats" in K.dbg:
            Bx = self.snap()

            def f_dbgp(e):
                e.wait_ge(S["vec"].h, Bx["vec"])
                S["st"].inc(e.dma_start(out=K.dbg["p0stats"].ap()[:, :],
                                        in_=K.fa[:, 17440:17952]), 16)
                e.wait_ge(S["st"].h, S["st"].c)
            K.blk.gpsimd(f_dbgp)

        if "xnT" in K.dbg:
            Bx = self.snap()

            def f_dbgst(e):
                e.wait_ge(S["gp"].h, Bx["gp"])
                for kc in range(8):
                    S["st"].inc(e.dma_start(out=K.dbg["xnT"].ap()[kc, :, :],
                                            in_=K.xnT_hbm.ap()[kc, :, :]), 16)
                e.wait_ge(S["st"].h, S["st"].c)
            K.blk.sync(f_dbgst)
        if "amax" in K.dbg:
            Bx = self.snap()

            def f_dbga(e):
                e.wait_ge(S["cc"].h, Bx["cc"])
                S["st"].inc(e.dma_start(out=K.dbg["amax"].ap()[:, :],
                                        in_=K.amx_out.ap()[:, :]), 16)
                e.wait_ge(S["st"].h, S["st"].c)
            K.blk.gpsimd(f_dbga)

    def ph_pw(self):
        K = self
        S = K.S
        ld, te, vec, sc, gp, fv = (S[n] for n in ["ld", "te", "vec", "sc", "gp", "fv"])
        B = self.snap()

        chunks = []
        for nm, src, nch, w in [("wop", K.w_op_T, 16, 1024), ("wq", K.w_q_T, 8, 1024),
                                ("wk", K.w_k_T, 8, 1024), ("wp", K.w_p_T, 8, 1024),
                                ("wf", K.w_f_T, 8, 1024), ("wd1", K.w_d1_T, 8, 4096),
                                ("wd2", K.w_d2_T, 32, 1024)]:
            hh, ll = K.wsp[nm]
            for c in range(nch):
                chunks.append((src, c, w, hh, ll))
        NCH = len(chunks)

        def f_sync(e):
            e.wait_ge(vec.h, B["vec"])
            e.wait_ge(gp.h, B["gp"])
            for i, (src, c, w, hh, ll) in enumerate(chunks):
                if i >= 2:
                    e.wait_ge(vec.h, B["vec"] + (i - 1))
                par = i % 2
                ld.inc(e.dma_start(out=K.fa[:, par * 4096:par * 4096 + w],
                                   in_=src.ap()[c * 128:(c + 1) * 128, :]), 16)

        def f_vector(e):
            for i, (src, c, w, hh, ll) in enumerate(chunks):
                e.wait_ge(ld.h, B["ld"] + 16 * (i + 1))
                if i >= 2:
                    e.wait_ge(gp.h, B["gp"] + 32 * (i - 1))
                par = i % 2
                srcw = K.fa[:, par * 4096:par * 4096 + w]
                hb = K.ba[:, par * 8192:par * 8192 + w]
                lb = K.ba[:, par * 8192 + 4096:par * 8192 + 4096 + w]
                tmp = K.fa[:, 8704 + par * 4096:8704 + par * 4096 + w]
                e.tensor_copy(out=hb, in_=srcw)
                e.tensor_copy(out=tmp, in_=hb)
                e.tensor_sub(out=tmp, in0=srcw, in1=tmp)
                vec.inc(e.tensor_copy(out=lb, in_=tmp), 1)

        def f_gpsimd(e):
            for i, (src, c, w, hh, ll) in enumerate(chunks):
                e.wait_ge(vec.h, B["vec"] + (i + 1))
                par = i % 2
                gp.inc(e.dma_start(out=hh.ap()[c * 128:(c + 1) * 128, :],
                                   in_=K.ba[:, par * 8192:par * 8192 + w]), 16)
                gp.inc(e.dma_start(out=ll.ap()[c * 128:(c + 1) * 128, :],
                                   in_=K.ba[:, par * 8192 + 4096:par * 8192 + 4096 + w]), 16)

        K.blk.sync(f_sync)
        K.blk.vector(f_vector)
        K.blk.gpsimd(f_gpsimd)

        B = self.snap()
        CSB = 17000

        def g_sync(e):
            e.wait_ge(vec.h, B["vec"])
            e.wait_ge(gp.h, B["gp"])
            for jc in range(32):
                if jc >= 2:
                    e.wait_ge(vec.h, B["vec"] + (jc - 1))
                par = jc % 2
                ld.inc(e.dma_start(out=K.fa[:, par * 4096:par * 4096 + 1024],
                                   in_=K.w_fi_T.ap()[jc * 128:(jc + 1) * 128, :]), 16)

        def g_vector(e):
            for jc in range(32):
                e.wait_ge(ld.h, B["ld"] + 16 * (jc + 1))
                par = jc % 2
                vec.inc(e.tensor_reduce(out=K.fa[:, CSB + jc:CSB + jc + 1],
                                        in_=K.fa[:, par * 4096:par * 4096 + 1024],
                                        axis=AX.X, op=ALU.max,
                                        apply_absolute_value=True), 1)
            def vfence(ee):
                t = fv.c
                fv.inc(ee.engine_nop(), 1)
                ee.wait_ge(fv.h, t + 1)
            vfence(e)
            e.tensor_scalar(out=K.fa[:, CSB:CSB + 32], in0=K.fa[:, CSB:CSB + 32],
                            scalar1=1e-8, scalar2=None, op0=ALU.max)
            vfence(e)
            e.reciprocal(out=K.fa[:, CSB + 40:CSB + 72], in_=K.fa[:, CSB:CSB + 32])
            vfence(e)
            vec.inc(e.tensor_scalar(out=K.fa[:, CSB + 40:CSB + 72],
                                    in0=K.fa[:, CSB + 40:CSB + 72],
                                    scalar1=127.0, scalar2=None, op0=ALU.mult), 1)
            e.wait_ge(te.h, B["te"] + 1)
            vec.inc(e.tensor_copy(out=K.fa[:32, CSB + 80:CSB + 208],
                                  in_=K.ps_a[:32, 0:128]), 1)

        def g_tensor(e):
            e.wait_ge(vec.h, B["vec"] + 33)
            te.inc(e.transpose(out=K.ps_a[0:32, 0:128],
                               in_=K.fa[:, CSB + 40:CSB + 72],
                               identity=K.ident_s[:]), 1)

        def g_gpsimd(e):
            e.wait_ge(vec.h, B["vec"] + 34)
            gp.inc(e.dma_start(out=K.csrow_hbm.ap().rearrange("o (c t) -> o c t", c=32)[0],
                               in_=K.fa[:32, CSB + 80:CSB + 208]), 16)

        K.blk.sync(g_sync)
        K.blk.vector(g_vector)
        K.blk.tensor(g_tensor)
        K.blk.gpsimd(g_gpsimd)

        B = self.snap()

        def h_sync(e):
            e.wait_ge(gp.h, B["gp"])
            e.wait_ge(vec.h, B["vec"])
            ld.inc(e.dma_start(out=K.fa[:, 12288:12288 + 4096],
                               in_=brow(K.csrow_hbm.ap()[0:1, :])), 16)
            for kc in range(8):
                if kc >= 2:
                    e.wait_ge(vec.h, B["vec"] + (kc - 1))
                par = kc % 2
                ld.inc(e.dma_start(out=K.fa[:, par * 4096:par * 4096 + 4096],
                                   in_=K.w_fi.ap()[kc * 128:(kc + 1) * 128, :]), 16)

        def h_vector(e):
            for kc in range(8):
                e.wait_ge(ld.h, B["ld"] + 16 * (kc + 2))
                if kc >= 2:
                    e.wait_ge(gp.h, B["gp"] + 16 * (kc - 1))
                par = kc % 2
                w = K.fa[:, par * 4096:par * 4096 + 4096]
                e.tensor_mul(out=K.fa[:, 8192:8192 + 4096], in0=w,
                             in1=K.fa[:, 12288:12288 + 4096])
                e.tensor_copy(out=K.ia[:, 0:2048], in_=K.fa[:, 8192:8192 + 2048])
                e.tensor_copy(out=K.ia[:, 2048:4096], in_=K.fa[:, 8192 + 2048:8192 + 4096])
                e.tensor_copy(out=K.ba[:, par * 8192:par * 8192 + 2048],
                              in_=K.ia[:, 0:2048])
                vec.inc(e.tensor_copy(out=K.ba[:, par * 8192 + 2048:par * 8192 + 4096],
                                      in_=K.ia[:, 2048:4096]), 1)

        def h_gpsimd(e):
            for kc in range(8):
                e.wait_ge(vec.h, B["vec"] + (kc + 1))
                par = kc % 2
                gp.inc(e.dma_start(out=K.wqi_hbm.ap()[kc * 128:(kc + 1) * 128, :],
                                   in_=K.ba[:, par * 8192:par * 8192 + 4096]), 16)

        K.blk.sync(h_sync)
        K.blk.vector(h_vector)
        K.blk.gpsimd(h_gpsimd)

        B = self.snap()
        WOS = 17100

        def i_sync(e):
            e.wait_ge(vec.h, B["vec"])
            e.wait_ge(gp.h, B["gp"])
            for kc in range(32):
                if kc >= 2:
                    e.wait_ge(vec.h, B["vec"] + (kc - 1))
                par = kc % 2
                ld.inc(e.dma_start(out=K.fa[:, par * 4096:par * 4096 + 1024],
                                   in_=K.w_fo.ap()[kc * 128:(kc + 1) * 128, :]), 16)
            for kc in range(32):
                e.wait_ge(vec.h, B["vec"] + 33 + max(0, kc - 1))
                par = kc % 2
                ld.inc(e.dma_start(out=K.fa[:, par * 4096:par * 4096 + 1024],
                                   in_=K.w_fo.ap()[kc * 128:(kc + 1) * 128, :]), 16)

        def i_vector(e):
            for kc in range(32):
                e.wait_ge(ld.h, B["ld"] + 16 * (kc + 1))
                par = kc % 2
                vec.inc(e.tensor_reduce(out=K.fa[:, WOS + kc:WOS + kc + 1],
                                        in_=K.fa[:, par * 4096:par * 4096 + 1024],
                                        axis=AX.X, op=ALU.max,
                                        apply_absolute_value=True), 1)

            def vfence(ee):
                t = fv.c
                fv.inc(ee.engine_nop(), 1)
                ee.wait_ge(fv.h, t + 1)
            vfence(e)
            e.tensor_scalar(out=K.fa[:, WOS:WOS + 32], in0=K.fa[:, WOS:WOS + 32],
                            scalar1=1e-8, scalar2=None, op0=ALU.max)
            vfence(e)
            e.tensor_scalar(out=K.consts[:, K.CO_WOS:K.CO_WOS + 32],
                            in0=K.fa[:, WOS:WOS + 32],
                            scalar1=1.0 / 127.0, scalar2=None, op0=ALU.mult)
            e.reciprocal(out=K.fa[:, WOS + 40:WOS + 72], in_=K.fa[:, WOS:WOS + 32])
            vfence(e)
            vec.inc(e.tensor_scalar(out=K.fa[:, WOS + 40:WOS + 72],
                                    in0=K.fa[:, WOS + 40:WOS + 72],
                                    scalar1=127.0, scalar2=None, op0=ALU.mult), 1)
            for kc in range(32):
                e.wait_ge(ld.h, B["ld"] + 16 * (32 + kc + 1))
                if kc >= 2:
                    e.wait_ge(gp.h, B["gp"] + 16 * (kc - 1))
                par = kc % 2
                w = K.fa[:, par * 4096:par * 4096 + 1024]
                e.tensor_scalar(out=K.fa[:, 8192:8192 + 1024], in0=w,
                                scalar1=K.fa[:, WOS + 40 + kc:WOS + 41 + kc],
                                scalar2=None, op0=ALU.mult)
                e.tensor_copy(out=K.ia[:, 0:1024], in_=K.fa[:, 8192:8192 + 1024])
                vec.inc(e.tensor_copy(out=K.ba[:, par * 8192:par * 8192 + 1024],
                                      in_=K.ia[:, 0:1024]), 1)

        def i_gpsimd(e):
            for kc in range(32):
                e.wait_ge(vec.h, B["vec"] + 33 + (kc + 1))
                par = kc % 2
                gp.inc(e.dma_start(out=K.wqo_hbm.ap()[kc * 128:(kc + 1) * 128, :],
                                   in_=K.ba[:, par * 8192:par * 8192 + 1024]), 16)

        K.blk.sync(i_sync)
        K.blk.vector(i_vector)
        K.blk.gpsimd(i_gpsimd)

        taps = [("wqi", K.wqi_hbm), ("wqo", K.wqo_hbm),
                ("wd1h", K.wsp["wd1"][0]), ("wd1l", K.wsp["wd1"][1])]
        if any(t in K.dbg for t, _ in taps) or "wos" in K.dbg:
            Bx = self.snap()

            def t_gp(e):
                e.wait_ge(gp.h, Bx["gp"])
                e.wait_ge(vec.h, Bx["vec"])
                for t, hbm in taps:
                    if t in K.dbg:
                        n = hbm.shape[0] // 128
                        for c in range(n):
                            S["st"].inc(e.dma_start(
                                out=K.dbg[t].ap()[c * 128:(c + 1) * 128, :],
                                in_=hbm.ap()[c * 128:(c + 1) * 128, :]), 16)
                if "wos" in K.dbg:
                    S["st"].inc(e.dma_start(
                        out=K.dbg["wos"].ap()[:, :],
                        in_=K.consts[:, K.CO_WOS:K.CO_WOS + 32]), 16)
                e.wait_ge(S["st"].h, S["st"].c)
            K.blk.gpsimd(t_gp)

    def ph_p1(self):
        K = self
        S = K.S
        ld, te, vec, sc, gp, cc, fv = (S[n] for n in
                                       ["ld", "te", "vec", "sc", "gp", "cc", "fv"])

        def wblk_src(m0):
            return AP(K.w_ip_T.ap().tensor, m0 * 128,
                      [[4096, 128], [128 * 4096, 8], [1, 128]])

        HAL, UB, CB, UC0, UC1, UD0, UD1, XP, XD = \
            2048, 4100, 6160, 8208, 10256, 12304, 14352, 16400, 2048

        B = self.snap()

        def a_sync(e):
            e.wait_ge(vec.h, B["vec"])
            e.wait_ge(gp.h, B["gp"])
            for m in range(16):
                if m >= 2:
                    e.wait_ge(vec.h, B["vec"] + (m - 1))
                ld.inc(e.dma_start(
                    out=K.fa[:, (m % 2) * 1024:(m % 2) * 1024 + 1024].rearrange(
                        "p (kc j) -> p kc j", kc=8),
                    in_=wblk_src(m)), 16)

        def a_vector(e):
            for m in range(16):
                e.wait_ge(ld.h, B["ld"] + 16 * (m + 1))
                if m >= 2:
                    e.wait_ge(te.h, B["te"] + (m - 1))
                par = m % 2
                srcw = K.fa[:, par * 1024:par * 1024 + 1024]
                hb = K.ba[:, par * 2048:par * 2048 + 1024]
                lb = K.ba[:, par * 2048 + 1024:par * 2048 + 2048]
                tmp = K.fa[:, 17000 + par * 1024:17000 + par * 1024 + 1024]
                e.tensor_copy(out=hb, in_=srcw)
                e.tensor_copy(out=tmp, in_=hb)
                e.tensor_sub(out=tmp, in0=srcw, in1=tmp)
                vec.inc(e.tensor_copy(out=lb, in_=tmp), 1)
            e.wait_ge(te.h, B["te"] + 16)
            vec.inc(e.tensor_copy(out=K.fa[:, HAL:HAL + 2048],
                                  in_=K.ps_a[:, 0:2048]), 1)

        def a_tensor(e):
            for m in range(16):
                e.wait_ge(vec.h, B["vec"] + (m + 1))
                par = m % 2
                hb = K.ba[:, par * 2048:par * 2048 + 1024]
                lb = K.ba[:, par * 2048 + 1024:par * 2048 + 2048]
                dst = K.ps_a[:, m * 128:(m + 1) * 128]
                n = 0
                for k in range(8):
                    e.ldweights(hb[:, k * 128:(k + 1) * 128])
                    e.matmul(out=dst, lhsT=hb[:, k * 128:(k + 1) * 128],
                             rhs=K.xnT_hs[:, k * TT:k * TT + 128],
                             start=(k == 0), stop=False)
                    e.matmul(out=dst, lhsT=hb[:, k * 128:(k + 1) * 128],
                             rhs=K.xnT_ls[:, k * TT:k * TT + 128],
                             start=False, stop=False)
                for k in range(8):
                    e.ldweights(lb[:, k * 128:(k + 1) * 128])
                    ins = e.matmul(out=dst, lhsT=lb[:, k * 128:(k + 1) * 128],
                                   rhs=K.xnT_hs[:, k * TT:k * TT + 128],
                                   start=False, stop=(k == 7))
                te.inc(ins, 1)

        K.blk.sync(a_sync)
        K.blk.vector(a_vector)
        K.blk.tensor(a_tensor)

        B = self.snap()
        VCS = [0]
        TCS = [0]
        SCS = [0]
        GCS = [0]
        for m in range(32):
            VCS.append(VCS[-1] + 3)
            TCS.append(TCS[-1] + 1)
            SCS.append(SCS[-1] + 1)
            GCS.append(GCS[-1] + (32 if m < 16 else 16))

        def b_sync(e):
            e.wait_ge(vec.h, B["vec"])
            for m in range(32):
                if m >= 2:
                    e.wait_ge(vec.h, B["vec"] + VCS[m - 1] + 1)
                ld.inc(e.dma_start(
                    out=K.fa[:, (m % 2) * 1024:(m % 2) * 1024 + 1024].rearrange(
                        "p (kc j) -> p kc j", kc=8),
                    in_=wblk_src(m)), 16)

        def b_vector(e):
            for m in range(32):
                par = m % 2
                e.wait_ge(ld.h, B["ld"] + 16 * (m + 1))
                e.wait_ge(te.h, B["te"] + TCS[max(0, m - 1)])
                srcw = K.fa[:, par * 1024:par * 1024 + 1024]
                hb = K.ba[:, par * 2048:par * 2048 + 1024]
                lb = K.ba[:, par * 2048 + 1024:par * 2048 + 2048]
                tmp = K.fa[:, 17000 + par * 1024:17000 + par * 1024 + 1024]
                e.tensor_copy(out=hb, in_=srcw)
                e.tensor_copy(out=tmp, in_=hb)
                e.tensor_sub(out=tmp, in0=srcw, in1=tmp)
                vec.inc(e.tensor_copy(out=lb, in_=tmp), 1)
                if m >= 1:
                    e.wait_ge(sc.h, B["sc"] + SCS[m])
                e.wait_ge(te.h, B["te"] + TCS[m + 1])
                ps = K.ps_a if par == 0 else K.ps_b
                if m < 16:
                    e.tensor_copy(out=K.fa[:, UB:UB + 3],
                                  in_=K.fa[:, HAL + m * 128 + 125:HAL + m * 128 + 128])
                    vec.inc(e.tensor_copy(out=K.fa[:, UB + 3:UB + 2051],
                                          in_=ps[:, 0:2048]), 1)
                    fvt = fv.c
                    fv.inc(e.engine_nop(), 1)
                    e.wait_ge(fv.h, fvt + 1)
                    e.tensor_scalar(out=K.fa[:, CB:CB + 2048],
                                    in0=K.fa[:, UB:UB + 2048],
                                    scalar1=K.consts[:, K.CO_CW + m * 4:K.CO_CW + m * 4 + 1],
                                    scalar2=K.consts[:, K.CO_CB + m:K.CO_CB + m + 1],
                                    op0=ALU.mult, op1=ALU.add)
                    for kk in range(1, 4):
                        e.scalar_tensor_tensor(
                            out=K.fa[:, CB:CB + 2048],
                            in0=K.fa[:, UB + kk:UB + kk + 2048],
                            scalar=K.consts[:, K.CO_CW + m * 4 + kk:K.CO_CW + m * 4 + kk + 1],
                            in1=K.fa[:, CB:CB + 2048],
                            op0=ALU.mult, op1=ALU.add)
                    vec.inc(e.engine_nop(), 1)
                else:
                    vec.inc(e.tensor_copy(out=K.fa[:, UB + 3:UB + 2051],
                                          in_=ps[:, 0:2048]), 1)
                    vec.inc(e.engine_nop(), 1)

        def b_scalar(e):
            for m in range(32):
                par2 = (m % 2)
                e.wait_ge(vec.h, B["vec"] + VCS[m + 1])
                if m >= 2:
                    e.wait_ge(gp.h, B["gp"] + GCS[m - 1])
                uc = K.fa[:, (UC0 if par2 == 0 else UC1):(UC0 if par2 == 0 else UC1) + 2048]
                src = K.fa[:, CB:CB + 2048] if m < 16 else K.fa[:, UB + 3:UB + 2051]
                sc.inc(e.activation(out=uc, in_=src, func=AF.Silu), 1)

        def b_vector2(e):
            pass

        def b_gpsimd(e):
            for m in range(32):
                par2 = (m % 2)
                e.wait_ge(sc.h, B["sc"] + SCS[m + 1])
                uc = K.fa[:, (UC0 if par2 == 0 else UC1):(UC0 if par2 == 0 else UC1) + 2048]
                if m < 16:
                    gp.inc(e.dma_start(out=K.u_hbm.ap()[m], in_=uc), 16)
                    ud = K.fa[:, (UD0 if par2 == 0 else UD1):(UD0 if par2 == 0 else UD1) + 2048]
                    e.tensor_scalar(out=ud, in0=uc,
                                    scalar1=K.consts[:, K.CO_DSK + m:K.CO_DSK + m + 1],
                                    scalar2=None, op0=ALU.mult)
                    gp.inc(e.dma_start(out=K.uD_hbm.ap()[m], in_=ud), 16)
                else:
                    gp.inc(e.dma_start(out=K.sres_hbm.ap()[m - 16], in_=uc), 16)

        def b_tensor(e):
            for m in range(32):
                par = m % 2
                e.wait_ge(vec.h, B["vec"] + VCS[m] + 1)
                if m >= 2:
                    e.wait_ge(vec.h, B["vec"] + VCS[m - 1] + 2)
                hb = K.ba[:, par * 2048:par * 2048 + 1024]
                lb = K.ba[:, par * 2048 + 1024:par * 2048 + 2048]
                ps = K.ps_a if par == 0 else K.ps_b
                for k in range(8):
                    e.ldweights(hb[:, k * 128:(k + 1) * 128])
                    for nq in range(4):
                        e.matmul(out=ps[:, nq * 512:(nq + 1) * 512],
                                 lhsT=hb[:, k * 128:(k + 1) * 128],
                                 rhs=K.xnT_hs[:, k * TT + 128 + nq * 512:
                                              k * TT + 128 + (nq + 1) * 512],
                                 start=(k == 0), stop=False)
                    for nq in range(4):
                        e.matmul(out=ps[:, nq * 512:(nq + 1) * 512],
                                 lhsT=hb[:, k * 128:(k + 1) * 128],
                                 rhs=K.xnT_ls[:, k * TT + 128 + nq * 512:
                                              k * TT + 128 + (nq + 1) * 512],
                                 start=False, stop=False)
                for k in range(8):
                    e.ldweights(lb[:, k * 128:(k + 1) * 128])
                    for nq in range(4):
                        ins = e.matmul(out=ps[:, nq * 512:(nq + 1) * 512],
                                       lhsT=lb[:, k * 128:(k + 1) * 128],
                                       rhs=K.xnT_hs[:, k * TT + 128 + nq * 512:
                                                    k * TT + 128 + (nq + 1) * 512],
                                       start=False, stop=(k == 7))
                te.inc(ins, 1)

        K.blk.sync(b_sync)
        K.blk.vector(b_vector)
        K.blk.scalar(b_scalar)
        K.blk.tensor(b_tensor)
        K.blk.gpsimd(b_gpsimd)

        B = self.snap()

        def c_sync(e):
            e.wait_ge(vec.h, B["vec"])
            e.wait_ge(gp.h, B["gp"])
            ld.inc(e.dma_start(out=K.fa[:, XP:XP + 1536].rearrange(
                "p (c j) -> p c j", c=16), in_=chunk_cols(K.w_xp_T, 16, 96)), 16)
            for m in range(16):
                if m >= 2:
                    e.wait_ge(vec.h, B["vec"] + 2 + (m - 1))
                ld.inc(e.dma_start(out=K.fa[:, (m % 2) * 2048:(m % 2) * 2048 + 2048],
                                   in_=K.u_hbm.ap()[m]), 16)

        def c_vector(e):
            e.wait_ge(ld.h, B["ld"] + 16)
            hb = K.ba[:, 8192:8192 + 1536]
            lb = K.ba[:, 9728:9728 + 1536]
            tmp = K.fa[:, UD0:UD0 + 1536]
            e.tensor_copy(out=hb, in_=K.fa[:, XP:XP + 1536])
            e.tensor_copy(out=tmp, in_=hb)
            e.tensor_sub(out=tmp, in0=K.fa[:, XP:XP + 1536], in1=tmp)
            vec.inc(e.tensor_copy(out=lb, in_=tmp), 1)
            vec.inc(e.engine_nop(), 1)
            for m in range(16):
                par = m % 2
                e.wait_ge(ld.h, B["ld"] + 16 * (m + 2))
                if m >= 2:
                    e.wait_ge(te.h, B["te"] + (m - 1))
                srcu = K.fa[:, par * 2048:par * 2048 + 2048]
                uh = K.ba[:, par * 4096:par * 4096 + 2048]
                ul = K.ba[:, par * 4096 + 2048:par * 4096 + 4096]
                tmp2 = K.fa[:, 17000 + par * 1024:17000 + par * 1024 + 1024]
                for hh in range(2):
                    sl = slice(hh * 1024, (hh + 1) * 1024)
                    e.tensor_copy(out=uh[:, sl], in_=srcu[:, sl])
                    e.tensor_copy(out=tmp2, in_=uh[:, sl])
                    e.tensor_sub(out=tmp2, in0=srcu[:, sl], in1=tmp2)
                    e.tensor_copy(out=ul[:, sl], in_=tmp2)
                vec.inc(e.engine_nop(), 1)
            e.wait_ge(te.h, B["te"] + 16)
            vec.inc(e.tensor_copy(out=K.fa[:96, XD:XD + 2048],
                                  in_=K.ps_b[:96, 0:2048]), 1)
            dth = K.ba[:64, 11264:11264 + 2048]
            dtl = K.ba[:64, 13312:13312 + 2048]
            tmp3 = K.fa[:64, 17000:17000 + 2048]
            e.tensor_copy(out=dth, in_=K.fa[:64, XD:XD + 2048])
            e.tensor_copy(out=tmp3, in_=dth)
            e.tensor_sub(out=tmp3, in0=K.fa[:64, XD:XD + 2048], in1=tmp3)
            vec.inc(e.tensor_copy(out=dtl, in_=tmp3), 1)

        def c_tensor(e):
            e.wait_ge(vec.h, B["vec"] + 1)
            for m in range(16):
                par = m % 2
                e.wait_ge(vec.h, B["vec"] + 2 + (m + 1))
                xph = K.ba[:, 8192 + m * 96:8192 + (m + 1) * 96]
                xpl = K.ba[:, 9728 + m * 96:9728 + (m + 1) * 96]
                uh = K.ba[:, par * 4096:par * 4096 + 2048]
                ul = K.ba[:, par * 4096 + 2048:par * 4096 + 4096]
                e.ldweights(xph)
                for nq in range(4):
                    e.matmul(out=K.ps_b[:96, nq * 512:(nq + 1) * 512], lhsT=xph,
                             rhs=uh[:, nq * 512:(nq + 1) * 512],
                             start=(m == 0), stop=False)
                for nq in range(4):
                    e.matmul(out=K.ps_b[:96, nq * 512:(nq + 1) * 512], lhsT=xph,
                             rhs=ul[:, nq * 512:(nq + 1) * 512],
                             start=False, stop=False)
                e.ldweights(xpl)
                for nq in range(4):
                    ins = e.matmul(out=K.ps_b[:96, nq * 512:(nq + 1) * 512], lhsT=xpl,
                                   rhs=uh[:, nq * 512:(nq + 1) * 512],
                                   start=False, stop=(m == 15))
                te.inc(ins, 1)

        def c_gpsimd(e):
            e.wait_ge(vec.h, B["vec"] + 2 + 16 + 1)
            if "xdbl" in K.dbg:
                gp.inc(e.dma_start(out=K.dbg["xdbl"].ap()[:96, :],
                                   in_=K.fa[:96, XD:XD + 2048]), 16)
            gpt = gp.c
            gp.inc(e.dma_start(out=K.bc_in.ap(), in_=K.fa[64:96, XD:XD + 2048]), 16)
            e.wait_ge(gp.h, gpt + 16)
            cc.inc(e.collective_compute(
                "AllGather", ALU.bypass, replica_groups=[list(range(NCORE))],
                ins=[K.bc_in.ap().opt()], outs=[K.bc_out.ap().opt()]), 1)

        K.blk.sync(c_sync)
        K.blk.vector(c_vector)
        K.blk.tensor(c_tensor)
        K.blk.gpsimd(c_gpsimd)

        B = self.snap()

        def d_sync(e):
            e.wait_ge(vec.h, B["vec"])
            e.wait_ge(gp.h, B["gp"])
            ld.inc(e.dma_start(out=K.fa[:64, CB:CB + 2048],
                               in_=K.w_dt_T.ap()[:, :]), 16)
            for m in range(16):
                if m >= 2:
                    e.wait_ge(vec.h, B["vec"] + 1 + 3 * (m - 1))
                ld.inc(e.dma_start(out=K.fa[:, (m % 2) * 2048:(m % 2) * 2048 + 2048],
                                   in_=K.u_hbm.ap()[m]), 16)

        def d_vector(e):
            e.wait_ge(ld.h, B["ld"] + 16)
            wh = K.ba[:64, 0:2048]
            wl = K.ba[:64, 2048:4096]
            tmp = K.fa[:64, 17000:17000 + 2048]
            e.tensor_copy(out=wh, in_=K.fa[:64, CB:CB + 2048])
            e.tensor_copy(out=tmp, in_=wh)
            e.tensor_sub(out=tmp, in0=K.fa[:64, CB:CB + 2048], in1=tmp)
            vec.inc(e.tensor_copy(out=wl, in_=tmp), 1)
            for m in range(16):
                par = m % 2
                e.wait_ge(te.h, B["te"] + (m + 1))
                if m >= 1:
                    e.wait_ge(sc.h, B["sc"] + m)
                ps = K.ps_a if par == 0 else K.ps_b
                e.tensor_scalar(out=K.fa[:, UB:UB + 2048], in0=ps[:, 0:2048],
                                scalar1=K.consts[:, K.CO_DTB + m:K.CO_DTB + m + 1],
                                scalar2=None, op0=ALU.add)
                vec.inc(e.engine_nop(), 1)
                e.wait_ge(sc.h, B["sc"] + (m + 1))
                e.wait_ge(ld.h, B["ld"] + 16 * (m + 2))
                if m >= 1:
                    e.wait_ge(gp.h, B["gp"] + 32 * m)
                e.tensor_add(out=K.fa[:, UD0:UD0 + 2048],
                             in0=K.fa[:, UC0:UC0 + 2048],
                             in1=K.fa[:, UC1:UC1 + 2048])
                e.tensor_mul(out=K.fa[:, UD1:UD1 + 2048],
                             in0=K.fa[:, UD0:UD0 + 2048],
                             in1=K.fa[:, par * 2048:par * 2048 + 2048])
                vec.inc(e.engine_nop(), 1)
                vec.inc(e.engine_nop(), 1)

        def d_scalar(e):
            for m in range(16):
                e.wait_ge(vec.h, B["vec"] + 1 + 3 * m + 1)
                z = K.fa[:, UB:UB + 2048]
                t1 = K.fa[:, UC0:UC0 + 2048]
                t2 = K.fa[:, UC1:UC1 + 2048]
                e.activation(out=t1, in_=z, func=AF.Abs)
                e.activation(out=t1, in_=t1, func=AF.Exp,
                             scale=K.consts[:, K.CC_NEG1:K.CC_NEG1 + 1])
                e.activation(out=t1, in_=t1, func=AF.Ln,
                             bias=K.consts[:, K.CC_ONE:K.CC_ONE + 1])
                sc.inc(e.activation(out=t2, in_=z, func=AF.Relu), 1)

        def d_tensor(e):
            e.wait_ge(vec.h, B["vec"] + 1)
            for m in range(16):
                par = m % 2
                if m >= 2:
                    e.wait_ge(vec.h, B["vec"] + 1 + 3 * (m - 2) + 1)
                ps = K.ps_a if par == 0 else K.ps_b
                wh = K.ba[:64, 0:2048]
                wl = K.ba[:64, 2048:4096]
                dth = K.ba[:64, 11264:11264 + 2048]
                dtl = K.ba[:64, 13312:13312 + 2048]
                e.ldweights(wh[:, m * 128:(m + 1) * 128])
                for nq in range(4):
                    e.matmul(out=ps[:, nq * 512:(nq + 1) * 512],
                             lhsT=wh[:, m * 128:(m + 1) * 128],
                             rhs=dth[:, nq * 512:(nq + 1) * 512],
                             start=True, stop=False)
                for nq in range(4):
                    e.matmul(out=ps[:, nq * 512:(nq + 1) * 512],
                             lhsT=wh[:, m * 128:(m + 1) * 128],
                             rhs=dtl[:, nq * 512:(nq + 1) * 512],
                             start=False, stop=False)
                e.ldweights(wl[:, m * 128:(m + 1) * 128])
                for nq in range(4):
                    ins = e.matmul(out=ps[:, nq * 512:(nq + 1) * 512],
                                   lhsT=wl[:, m * 128:(m + 1) * 128],
                                   rhs=dth[:, nq * 512:(nq + 1) * 512],
                                   start=False, stop=True)
                te.inc(ins, 1)

        def d_gpsimd(e):
            for m in range(16):
                e.wait_ge(vec.h, B["vec"] + 1 + 3 * m + 2)
                gp.inc(e.dma_start(
                    out=K.dd_in.ap()[m // 2, 0, (m % 2) * 128:(m % 2) * 128 + 128, :],
                    in_=K.fa[:, UD0:UD0 + 2048]), 16)
                gp.inc(e.dma_start(
                    out=K.dd_in.ap()[m // 2, 1, (m % 2) * 128:(m % 2) * 128 + 128, :],
                    in_=K.fa[:, UD1:UD1 + 2048]), 16)
            e.wait_ge(gp.h, gp.c)
            with allow_ici_all_to_all():
                cc.inc(e.collective_compute(
                    "AllToAll", ALU.bypass, replica_groups=[list(range(NCORE))],
                    ins=[K.dd_in.ap().opt()], outs=[K.dd_out.ap().opt()]), 1)

        K.blk.sync(d_sync)
        K.blk.vector(d_vector)
        K.blk.scalar(d_scalar)
        K.blk.tensor(d_tensor)
        K.blk.gpsimd(d_gpsimd)

        for tap, hbm, n in [("u", K.u_hbm, 16), ("dd", K.dd_out, None),
                            ("bc", K.bc_out, None), ("xdbl", None, None)]:
            if tap not in K.dbg:
                continue
            Bx = self.snap()

            def t_gp(e, tap=tap, hbm=hbm, Bx=Bx):
                e.wait_ge(gp.h, Bx["gp"])
                e.wait_ge(cc.h, Bx["cc"])
                e.wait_ge(vec.h, Bx["vec"])
                if tap == "u":
                    for c in range(16):
                        S["st"].inc(e.dma_start(out=K.dbg["u"].ap()[c],
                                                in_=K.u_hbm.ap()[c]), 16)
                elif tap == "dd":
                    for c in range(NCORE):
                        S["st"].inc(e.dma_start(out=K.dbg["dd"].ap()[c],
                                                in_=K.dd_out.ap()[c]), 16)
                elif tap == "bc":
                    for c in range(NCORE):
                        S["st"].inc(e.dma_start(out=K.dbg["bc"].ap()[c],
                                                in_=K.bc_out.ap()[c]), 16)
                elif tap == "xdbl":
                    pass
                e.wait_ge(S["st"].h, S["st"].c)
            K.blk.gpsimd(t_gp)

    def ph_p2(self):
        K = self
        S = K.S
        ld, vec, sc, gp, cc, fv = (S[n] for n in ["ld", "vec", "sc", "gp", "cc", "fv"])
        B = self.snap()
        DSUB, USUB, BB, CB2, DA, DB2, HB2, HC2, YT, CAR = \
            0, 2048, 4096, 5120, 6144, 7168, 8192, 9216, 10240, 18432

        def dsl(base, par):
            return K.fa[:, base + par * 512:base + par * 512 + 512]

        def vcum(k):
            return 32 * k

        def vbase(k):
            b = k // 8
            return B["vec"] + vcum(k) + (b + 1)

        def f_sync(e):
            e.wait_ge(vec.h, B["vec"])
            e.wait_ge(cc.h, B["cc"])
            for k in range(32):
                b, sb = k // 8, k % 8
                spar = sb % 2
                blk = 2 * b + sb // 4
                tq = 512 * (sb % 4)
                if k >= 2:
                    e.wait_ge(vec.h, vbase(k - 1))
                for dt in range(2):
                    ld.inc(e.dma_start(
                        out=dsl(DSUB, spar * 2 + dt),
                        in_=K.dd_out.ap()[blk, 0, dt * 128:(dt + 1) * 128,
                                          tq:tq + 512]), 16)
                    ld.inc(e.dma_start(
                        out=dsl(USUB, spar * 2 + dt),
                        in_=K.dd_out.ap()[blk, 1, dt * 128:(dt + 1) * 128,
                                          tq:tq + 512]), 16)
                for n in range(NST):
                    npar = n % 2
                    if n >= 2:
                        e.wait_ge(vec.h, vbase(k) + 2 * (n - 1))
                    elif k >= 1:
                        e.wait_ge(vec.h, vbase(k - 1) + 30 + 2 * n)
                    src_b = K.bc_out.ap()[blk, n:n + 1, tq:tq + 512]
                    src_c = K.bc_out.ap()[blk, 16 + n:17 + n, tq:tq + 512]
                    ld.inc(e.dma_start(out=dsl(BB, npar), in_=brow(src_b)), 16)
                    ld.inc(e.dma_start(out=dsl(CB2, npar), in_=brow(src_c)), 16)

        def f_scalar(e):
            for k in range(32):
                for j in range(32):
                    n, dt = j // 2, j % 2
                    jpar = j % 2
                    e.wait_ge(ld.h, B["ld"] + 16 * (36 * k + 4))
                    if k > 0 or j >= 2:
                        e.wait_ge(vec.h, vbase(k) + max(0, j - 1))
                    sc.inc(e.activation(
                        out=dsl(DA, jpar), in_=dsl(DSUB, (k % 2) * 2 + dt),
                        func=AF.Exp,
                        scale=K.consts[:, K.CO_A + dt * 16 + n:
                                       K.CO_A + dt * 16 + n + 1]), 1)

        def f_vector(e):
            for k in range(32):
                b, sb = k // 8, k % 8
                if sb == 0:
                    fvt = fv.c
                    e.memset(K.fa[:, CAR:CAR + 32], 0.0)
                    fv.inc(e.engine_nop(), 1)
                    e.wait_ge(fv.h, fvt + 1)
                    vec.inc(e.engine_nop(), 1)
                for j in range(32):
                    n, dt = j // 2, j % 2
                    jpar = j % 2
                    e.wait_ge(sc.h, B["sc"] + 32 * k + j + 1)
                    e.wait_ge(ld.h, B["ld"] + 16 * (36 * k + 4 + 2 * (n + 1)))
                    if k > 0 or j >= 2:
                        e.wait_ge(gp.h, B["gp"] + 32 * k + max(0, j - 1) +
                                  16 * 4 * b)
                    ccol = K.fa[:, CAR + dt * 16 + n:CAR + dt * 16 + n + 1]
                    e.tensor_mul(out=dsl(DB2, jpar),
                                 in0=dsl(USUB, (k % 2) * 2 + dt),
                                 in1=dsl(BB, n % 2))
                    e.tensor_tensor_scan(out=dsl(HB2, jpar), data0=dsl(DA, jpar),
                                         data1=dsl(DB2, jpar), initial=ccol,
                                         op0=ALU.mult, op1=ALU.add)
                    e.tensor_mul(out=dsl(HC2, jpar), in0=dsl(HB2, jpar),
                                 in1=dsl(CB2, n % 2))
                    vec.inc(e.tensor_copy(
                        out=ccol, in_=dsl(HB2, jpar)[:, 511:512]), 1)

        def f_gpsimd(e):
            for k in range(32):
                b, sb = k // 8, k % 8
                for j in range(32):
                    n, dt = j // 2, j % 2
                    jpar = j % 2
                    if sb == 0 and j == 0 and b >= 1:
                        e.wait_ge(gp.h, B["gp"] + 256 * b + 64 * b)
                    e.wait_ge(vec.h, vbase(k) + j + 1)
                    ydst = K.fa[:, YT + dt * 4096 + sb * 512:
                                YT + dt * 4096 + sb * 512 + 512]
                    if n == 0:
                        gp.inc(e.tensor_copy(out=ydst, in_=dsl(HC2, jpar)), 1)
                    else:
                        gp.inc(e.tensor_add(out=ydst, in0=ydst,
                                            in1=dsl(HC2, jpar)), 1)
                if sb == 7:
                    for dt in range(2):
                        for half in range(2):
                            gp.inc(e.dma_start(
                                out=K.y_in.ap()[2 * b + half,
                                                dt * 128:(dt + 1) * 128, :],
                                in_=K.fa[:, YT + dt * 4096 + half * 2048:
                                         YT + dt * 4096 + half * 2048 + 2048]), 16)
            e.wait_ge(gp.h, gp.c)
            with allow_ici_all_to_all():
                cc.inc(e.collective_compute(
                    "AllToAll", ALU.bypass, replica_groups=[list(range(NCORE))],
                    ins=[K.y_in.ap().opt()], outs=[K.y_out.ap().opt()]), 1)

        K.blk.sync(f_sync)
        K.blk.scalar(f_scalar)
        K.blk.vector(f_vector)
        K.blk.gpsimd(f_gpsimd)

        if "y" in K.dbg:
            Bx = self.snap()

            def t_gp(e):
                e.wait_ge(cc.h, Bx["cc"])
                for c in range(NCORE):
                    S["st"].inc(e.dma_start(out=K.dbg["y"].ap()[c],
                                            in_=K.y_out.ap()[c]), 16)
                e.wait_ge(S["st"].h, S["st"].c)
            K.blk.gpsimd(t_gp)

    def ph_ffn(self):
        K = self
        S = K.S
        ld, te, vec, sc, gp, cc, fv, ld3 = (S[n] for n in
                                            ["ld", "te", "vec", "sc", "gp", "cc",
                                             "fv", "ld3"])
        B = self.snap()
        CL = 17000
        HW = 4100
        HID = 8208

        def vfence(e):
            t = fv.c
            fv.inc(e.engine_nop(), 1)
            e.wait_ge(fv.h, t + 1)

        def a_sync(e):
            e.wait_ge(vec.h, B["vec"])
            e.wait_ge(gp.h, B["gp"])
            e.wait_ge(cc.h, B["cc"])
            ld.inc(e.dma_start(
                out=K.fa[:, CL:CL + 1],
                in_=AP(K.amx_out.ap().tensor, 0, [[0, 128], [1, 1]])), 16)
            for jm in range(32):
                ld.inc(e.dma_start(
                    out=K.fa[:, CL + 8 + jm:CL + 9 + jm],
                    in_=AP(K.csrow_hbm.ap().tensor, jm * 128, [[1, 128], [1, 1]])), 16)
            e.wait_ge(ld.h, B["ld"] + 16 * 33)
            for kc in range(8):
                if kc >= 2:
                    e.wait_ge(vec.h, B["vec"] + 1 + (kc - 1))
                e.wait_ge(ld.h, B["ld"] + 16 * (33 + kc))
                ld.inc(e.dma_start(out=K.fa[:, (kc % 2) * 2048:(kc % 2) * 2048 + 2048],
                                   in_=K.xnT_hbm.ap()[kc, :, 128:2176]), 16)

        def a_vector(e):
            e.wait_ge(ld.h, B["ld"] + 16 * 33)
            c = K.fa
            e.tensor_scalar(out=c[:, CL:CL + 1], in0=c[:, CL:CL + 1],
                            scalar1=1e-8, scalar2=None, op0=ALU.max)
            vfence(e)
            e.reciprocal(out=c[:, CL + 1:CL + 2], in_=c[:, CL:CL + 1])
            vfence(e)
            e.tensor_scalar(out=c[:, CL + 1:CL + 2], in0=c[:, CL + 1:CL + 2],
                            scalar1=32767.0, scalar2=None, op0=ALU.mult)
            e.tensor_scalar(out=c[:, CL + 2:CL + 3], in0=c[:, CL:CL + 1],
                            scalar1=1.0 / 32767.0, scalar2=None, op0=ALU.mult)
            e.reciprocal(out=c[:, CL + 40:CL + 72], in_=c[:, CL + 8:CL + 40])
            vfence(e)
            vec.inc(e.tensor_scalar(out=c[:, CL + 40:CL + 72],
                                    in0=c[:, CL + 40:CL + 72],
                                    scalar1=c[:, CL + 2:CL + 3], scalar2=None,
                                    op0=ALU.mult), 1)
            for kc in range(8):
                par = kc % 2
                e.wait_ge(ld.h, B["ld"] + 16 * (33 + kc + 1))
                xn = c[:, par * 2048:par * 2048 + 2048]
                e.tensor_scalar(out=c[:, HW:HW + 2048], in0=xn,
                                scalar1=c[:, CL + 1:CL + 2], scalar2=None,
                                op0=ALU.mult)
                e.tensor_copy(out=K.ia[:, 0:2048], in_=c[:, HW:HW + 2048])
                e.tensor_copy(out=c[:, HW:HW + 2048], in_=K.ia[:, 0:2048])
                hseg = K.xnT_hs[:, kc * 2048:(kc + 1) * 2048]
                lseg = K.xnT_ls[:, kc * 2048:(kc + 1) * 2048]
                e.tensor_copy(out=hseg, in_=c[:, HW:HW + 2048])
                e.tensor_copy(out=c[:, HW + 2048:HW + 4096], in_=hseg)
                e.tensor_sub(out=c[:, HW + 2048:HW + 4096], in0=c[:, HW:HW + 2048],
                             in1=c[:, HW + 2048:HW + 4096])
                vec.inc(e.tensor_copy(out=lseg, in_=c[:, HW + 2048:HW + 4096]), 1)

        K.blk.sync(a_sync)
        K.blk.vector(a_vector)

        B = self.snap()

        def b_sync(e):
            e.wait_ge(vec.h, B["vec"])
            for jm in range(32):
                if jm >= 2:
                    e.wait_ge(te.h, B["te"] + (jm - 1))
                e.wait_ge(ld.h, B["ld"] + 16 * 8 * jm)
                for kc2 in range(8):
                    ld.inc(e.dma_start(
                        out=K.ba[:, (jm % 2) * 1024 + kc2 * 128:
                                 (jm % 2) * 1024 + (kc2 + 1) * 128],
                        in_=K.wqi_hbm.ap()[kc2 * 128:(kc2 + 1) * 128,
                                           jm * 128:(jm + 1) * 128]), 16)

        def b_tensor(e):
            e.wait_ge(vec.h, B["vec"])
            for jm in range(32):
                par = jm % 2
                e.wait_ge(ld.h, B["ld"] + 16 * 8 * (jm + 1))
                if jm >= 2:
                    e.wait_ge(sc.h, B["sc"] + (jm - 1))
                ps = K.ps_a if par == 0 else K.ps_b
                wt = K.ba[:, par * 1024:par * 1024 + 1024]
                for kc in range(8):
                    e.ldweights(wt[:, kc * 128:(kc + 1) * 128])
                    for nq in range(4):
                        e.matmul(out=ps[:, nq * 512:(nq + 1) * 512],
                                 lhsT=wt[:, kc * 128:(kc + 1) * 128],
                                 rhs=K.xnT_hs[:, kc * 2048 + nq * 512:
                                              kc * 2048 + (nq + 1) * 512],
                                 start=(kc == 0), stop=False)
                    for nq in range(4):
                        ins = e.matmul(out=ps[:, nq * 512:(nq + 1) * 512],
                                       lhsT=wt[:, kc * 128:(kc + 1) * 128],
                                       rhs=K.xnT_ls[:, kc * 2048 + nq * 512:
                                                    kc * 2048 + (nq + 1) * 512],
                                       start=False, stop=(kc == 7))
                te.inc(ins, 1)

        def b_scalar(e):
            e.wait_ge(vec.h, B["vec"])
            for jm in range(32):
                par = jm % 2
                e.wait_ge(te.h, B["te"] + (jm + 1))
                if jm >= 2:
                    e.wait_ge(gp.h, B["gp"] + 16 * (jm - 1))
                dst = K.fa[:, HID + par * 2048:HID + par * 2048 + 2048]
                sc.inc(e.activation(
                    out=dst, in_=(K.ps_a if par == 0 else K.ps_b)[:, 0:2048],
                    func=AF.Gelu,
                    scale=K.fa[:, CL + 40 + jm:CL + 41 + jm],
                    bias=K.consts[:, K.CO_FIB + jm:K.CO_FIB + jm + 1]), 1)

        def b_vector(e):
            for jm in range(32):
                par = jm % 2
                e.wait_ge(sc.h, B["sc"] + (jm + 1))
                src = K.fa[:, HID + par * 2048:HID + par * 2048 + 2048]
                vec.inc(e.tensor_reduce(
                    out=K.fa[:, CL + 300 + jm:CL + 301 + jm], in_=src,
                    axis=AX.X, op=ALU.max, apply_absolute_value=True), 1)
            vfence(e)
            vec.inc(e.tensor_reduce(out=K.fa[:, CL + 3:CL + 4],
                                    in_=K.fa[:, CL + 300:CL + 332],
                                    axis=AX.X, op=ALU.max), 1)

        def b_gpsimd(e):
            for jm in range(32):
                par = jm % 2
                e.wait_ge(vec.h, B["vec"] + (jm + 1))
                e.wait_ge(gp.h, B["gp"] + 16 * jm)
                gp.inc(e.dma_start(out=K.hid_hbm.ap()[jm],
                                   in_=K.fa[:, HID + par * 2048:
                                            HID + par * 2048 + 2048]), 16)
            e.wait_ge(vec.h, B["vec"] + 33)
            gpt = gp.c
            gp.inc(e.dma_start(
                out=AP(K.hmx_in.ap().tensor, 0, [[1, 1], [1, 1]]),
                in_=K.fa[0:1, CL + 3:CL + 4]), 16)
            gp.inc(e.dma_start(
                out=AP(K.gar_in.ap().tensor, 0, [[1, 128]]),
                in_=K.fa[:, CL + 3:CL + 4]), 16)
            e.wait_ge(gp.h, gpt + 32)
            gp.inc(e.dma_start(
                out=K.fa[0:1, CL + 96:CL + 224],
                in_=AP(K.gar_in.ap().tensor, 0, [[128, 1], [1, 128]])), 16)

        def b_vector2(e):
            e.wait_ge(gp.h, B["gp"] + 16 * 35)
            vec.inc(e.tensor_reduce(out=K.fa[0:1, CL + 4:CL + 5],
                                    in_=K.fa[0:1, CL + 96:CL + 224],
                                    axis=AX.X, op=ALU.max), 1)

        def b_gpsimd2(e):
            e.wait_ge(vec.h, B["vec"] + 34)
            gpt = gp.c
            gp.inc(e.dma_start(out=K.hmx_in.ap()[0:1, 0:1],
                               in_=K.fa[0:1, CL + 4:CL + 5]), 16)
            e.wait_ge(gp.h, gpt + 16)
            cc.inc(e.collective_compute(
                "AllReduce", ALU.max, replica_groups=[list(range(NCORE))],
                ins=[K.hmx_in.ap().opt()], outs=[K.hmx_out.ap().opt()]), 1)

        K.blk.sync(b_sync)
        K.blk.tensor(b_tensor)
        K.blk.scalar(b_scalar)
        K.blk.vector(b_vector)
        K.blk.gpsimd(b_gpsimd)
        K.blk.vector(b_vector2)
        K.blk.gpsimd(b_gpsimd2)

        B = self.snap()

        def c_sync(e):
            e.wait_ge(cc.h, B["cc"])
            e.wait_ge(gp.h, B["gp"])
            ld.inc(e.dma_start(
                out=K.fa[:, CL + 5:CL + 6],
                in_=AP(K.hmx_out.ap().tensor, 0, [[0, 128], [1, 1]])), 16)
            e.wait_ge(ld.h, B["ld"] + 16)
            for kc in range(32):
                if kc >= 2:
                    e.wait_ge(vec.h, B["vec"] + 2 + (kc - 1))
                e.wait_ge(ld.h, B["ld"] + 16 * (kc + 1))
                ld.inc(e.dma_start(out=K.fa[:, (kc % 2) * 2048:(kc % 2) * 2048 + 2048],
                                   in_=K.hid_hbm.ap()[kc]), 16)

        def c_vector(e):
            c = K.fa
            e.wait_ge(ld.h, B["ld"] + 16)
            e.tensor_scalar(out=c[:, CL + 5:CL + 6], in0=c[:, CL + 5:CL + 6],
                            scalar1=1e-8, scalar2=None, op0=ALU.max)
            vfence(e)
            e.reciprocal(out=c[:, CL + 6:CL + 7], in_=c[:, CL + 5:CL + 6])
            vfence(e)
            e.tensor_scalar(out=c[:, CL + 6:CL + 7], in0=c[:, CL + 6:CL + 7],
                            scalar1=32767.0, scalar2=None, op0=ALU.mult)
            e.tensor_scalar(out=c[:, CL + 7:CL + 8], in0=c[:, CL + 5:CL + 6],
                            scalar1=1.0 / 32767.0, scalar2=None, op0=ALU.mult)
            vfence(e)
            e.tensor_scalar(out=c[:, CL + 128:CL + 160],
                            in0=K.consts[:, K.CO_WOS:K.CO_WOS + 32],
                            scalar1=c[:, CL + 7:CL + 8], scalar2=None, op0=ALU.mult)
            vfence(e)
            vec.inc(e.engine_nop(), 1)
            vec.inc(e.engine_nop(), 1)
            for kc in range(32):
                par = kc % 2
                e.wait_ge(ld.h, B["ld"] + 16 * (kc + 2))
                if kc >= 2:
                    e.wait_ge(gp.h, B["gp"] + 32 * (kc - 1))
                hidsrc = c[:, par * 2048:par * 2048 + 2048]
                e.tensor_scalar(out=c[:, HW:HW + 2048], in0=hidsrc,
                                scalar1=c[:, CL + 6:CL + 7], scalar2=None,
                                op0=ALU.mult)
                e.tensor_copy(out=K.ia[:, 0:2048], in_=c[:, HW:HW + 2048])
                e.tensor_copy(out=c[:, HW:HW + 2048], in_=K.ia[:, 0:2048])
                e.tensor_scalar(out=c[:, HW:HW + 2048], in0=c[:, HW:HW + 2048],
                                scalar1=c[:, CL + 128 + kc:CL + 129 + kc],
                                scalar2=None, op0=ALU.mult)
                hseg = K.ba[:, par * 4096:par * 4096 + 2048]
                lseg = K.ba[:, par * 4096 + 2048:par * 4096 + 4096]
                e.tensor_copy(out=hseg, in_=c[:, HW:HW + 2048])
                e.tensor_copy(out=c[:, HW + 2048:HW + 4096], in_=hseg)
                e.tensor_sub(out=c[:, HW + 2048:HW + 4096], in0=c[:, HW:HW + 2048],
                             in1=c[:, HW + 2048:HW + 4096])
                vec.inc(e.tensor_copy(out=lseg, in_=c[:, HW + 2048:HW + 4096]), 1)

        def c_gpsimd(e):
            for kc in range(32):
                par = kc % 2
                e.wait_ge(vec.h, B["vec"] + 2 + (kc + 1))
                e.wait_ge(gp.h, B["gp"] + 32 * kc)
                gp.inc(e.dma_start(out=K.hqs_h.ap()[kc],
                                   in_=K.ba[:, par * 4096:par * 4096 + 2048]), 16)
                gp.inc(e.dma_start(out=K.hqs_l.ap()[kc],
                                   in_=K.ba[:, par * 4096 + 2048:par * 4096 + 4096]), 16)

        K.blk.sync(c_sync)
        K.blk.vector(c_vector)
        K.blk.gpsimd(c_gpsimd)

        B = self.snap()

        def d_sync(e):
            e.wait_ge(vec.h, B["vec"])
            e.wait_ge(gp.h, B["gp"])
            ld.inc(e.dma_start(out=K.fa[:, HW:HW + 1024],
                               in_=brow(K.rows_in.ap()[R_FOB:R_FOB + 1, :])), 16)
            for mtp in range(8):
                for kc in range(32):
                    i = mtp * 32 + kc
                    semd = ld if i % 2 == 0 else ld3
                    base = B["ld"] + 16 if i % 2 == 0 else B["ld3"]
                    if i >= 4:
                        e.wait_ge(te.h, B["te"] + (i - 3))
                    e.wait_ge(semd.h, base + 48 * (i // 2))
                    par = i % 4
                    semd.inc(e.dma_start(
                        out=K.ba[:, par * 1024:par * 1024 + 1024],
                        in_=K.wqo_hbm.ap()[kc * 128:(kc + 1) * 128, :]), 16)
                    semd.inc(e.dma_start(
                        out=K.ba[:, 4096 + par * 512:4096 + par * 512 + 256],
                        in_=K.hqs_h.ap()[kc, :, mtp * 256:(mtp + 1) * 256]), 16)
                    semd.inc(e.dma_start(
                        out=K.ba[:, 4096 + par * 512 + 256:4096 + par * 512 + 512],
                        in_=K.hqs_l.ap()[kc, :, mtp * 256:(mtp + 1) * 256]), 16)

        def d_tensor(e):
            e.wait_ge(ld.h, B["ld"] + 16)
            e.wait_ge(sc.h, B["sc"])
            e.wait_ge(vec.h, B["vec"])
            for mtp in range(8):
                ppar = mtp % 2
                ps = K.ps_a if ppar == 0 else K.ps_b
                if mtp >= 2:
                    e.wait_ge(vec.h, B["vec"] + 2 * (mtp - 1))
                for kc in range(32):
                    i = mtp * 32 + kc
                    par = i % 4
                    e.wait_ge((ld if i % 2 == 0 else S["ld3"]).h,
                              (B["ld"] + 16 if i % 2 == 0 else B["ld3"])
                              + 48 * (i // 2 + 1))
                    wo = K.ba[:, par * 1024:par * 1024 + 1024]
                    hqh = K.ba[:, 4096 + par * 512:4096 + par * 512 + 256]
                    hql = K.ba[:, 4096 + par * 512 + 256:4096 + par * 512 + 512]
                    for half in range(2):
                        e.ldweights(hqh[:, half * 128:(half + 1) * 128])
                        for nq in range(2):
                            e.matmul(out=ps[:, half * 1024 + nq * 512:
                                            half * 1024 + (nq + 1) * 512],
                                     lhsT=hqh[:, half * 128:(half + 1) * 128],
                                     rhs=wo[:, nq * 512:(nq + 1) * 512],
                                     start=(kc == 0), stop=False)
                        e.ldweights(hql[:, half * 128:(half + 1) * 128])
                        for nq in range(2):
                            ins = e.matmul(out=ps[:, half * 1024 + nq * 512:
                                                  half * 1024 + (nq + 1) * 512],
                                           lhsT=hql[:, half * 128:(half + 1) * 128],
                                           rhs=wo[:, nq * 512:(nq + 1) * 512],
                                           start=False, stop=(kc == 31))
                    te.inc(ins, 1)

        def d_vector(e):
            e.wait_ge(ld.h, B["ld"] + 16)
            for mtp in range(8):
                ppar = mtp % 2
                ps = K.ps_a if ppar == 0 else K.ps_b
                e.wait_ge(te.h, B["te"] + 32 * (mtp + 1))
                if mtp >= 1:
                    e.wait_ge(sc.h, B["sc"] + 2 * mtp)
                for half in range(2):
                    vec.inc(e.tensor_add(
                        out=K.fa[:, HID + half * 1024:HID + (half + 1) * 1024],
                        in0=ps[:, half * 1024:(half + 1) * 1024],
                        in1=K.fa[:, HW:HW + 1024]), 1)

        def d_scalar(e):
            for mtp in range(8):
                e.wait_ge(vec.h, B["vec"] + 2 * (mtp + 1))
                if mtp >= 1:
                    e.wait_ge(gp.h, B["gp"] + 32 * mtp)
                for half in range(2):
                    sc.inc(e.activation(
                        out=K.fa[:, HID + 2048 + half * 1024:
                                 HID + 2048 + (half + 1) * 1024],
                        in_=K.fa[:, HID + half * 1024:HID + (half + 1) * 1024],
                        func=AF.Silu), 1)

        def d_gpsimd(e):
            for mtp in range(8):
                e.wait_ge(sc.h, B["sc"] + 2 * (mtp + 1))
                e.wait_ge(gp.h, B["gp"] + 32 * mtp)
                for half in range(2):
                    gp.inc(e.dma_start(
                        out=K.g_hbm.ap()[mtp * 2 + half],
                        in_=K.fa[:, HID + 2048 + half * 1024:
                                 HID + 2048 + (half + 1) * 1024]), 16)

        K.blk.sync(d_sync)
        K.blk.tensor(d_tensor)
        K.blk.vector(d_vector)
        K.blk.scalar(d_scalar)
        K.blk.gpsimd(d_gpsimd)

        if "xq" in K.dbg or "scol" in K.dbg:
            Bq = self.snap()

            def t_q(e):
                e.wait_ge(vec.h, Bq["vec"])
                if "xq" in K.dbg:
                    S["st"].inc(e.dma_start(out=K.dbg["xq"].ap()[:, :],
                                            in_=K.xnT_hs[:, 0:16384]), 16)
                if "scol" in K.dbg:
                    S["st"].inc(e.dma_start(out=K.dbg["scol"].ap()[:, :],
                                            in_=K.fa[:, 17000:17256]), 16)
                e.wait_ge(S["st"].h, S["st"].c)
            K.blk.gpsimd(t_q)

        for tap, hbm, n in [("hid", K.hid_hbm, 32), ("g", K.g_hbm, 16)]:
            if tap not in K.dbg:
                continue
            Bx = self.snap()

            def t_gp(e, tap=tap, hbm=hbm, n=n, Bx=Bx):
                e.wait_ge(gp.h, Bx["gp"])
                for c in range(n):
                    S["st"].inc(e.dma_start(out=K.dbg[tap].ap()[c],
                                            in_=hbm.ap()[c]), 16)
                e.wait_ge(S["st"].h, S["st"].c)
            K.blk.gpsimd(t_gp)

    def ph_p3a(self):
        K = self
        S = K.S
        ld, vec, gp, cc, fv = (S[n] for n in ["ld", "vec", "gp", "cc", "fv"])
        B = self.snap()

        def a_sync(e):
            e.wait_ge(cc.h, B["cc"])
            e.wait_ge(vec.h, B["vec"])
            e.wait_ge(gp.h, B["gp"])
            for kc in range(16):
                if kc >= 2:
                    e.wait_ge(vec.h, B["vec"] + (kc - 1))
                par = kc % 2
                ld.inc(e.dma_start(
                    out=K.fa[:, par * 2048:par * 2048 + 2048],
                    in_=K.y_out.ap()[kc // 2, (kc % 2) * 128:(kc % 2) * 128 + 128, :]
                    if False else K.y_out.ap()[kc // 2][(kc % 2) * 128:(kc % 2) * 128 + 128, :]), 16)
                ld.inc(e.dma_start(out=K.fa[:, 4100 + par * 2048:4100 + par * 2048 + 2048],
                                   in_=K.uD_hbm.ap()[kc]), 16)
                ld.inc(e.dma_start(out=K.fa[:, 8200 + par * 2048:8200 + par * 2048 + 2048],
                                   in_=K.sres_hbm.ap()[kc]), 16)

        def a_vector(e):
            for kc in range(16):
                par = kc % 2
                e.wait_ge(ld.h, B["ld"] + 16 * 3 * (kc + 1))
                if kc >= 2:
                    e.wait_ge(gp.h, B["gp"] + 32 * (kc - 1))
                y = K.fa[:, par * 2048:par * 2048 + 2048]
                ud = K.fa[:, 4100 + par * 2048:4100 + par * 2048 + 2048]
                sr = K.fa[:, 8200 + par * 2048:8200 + par * 2048 + 2048]
                w = K.fa[:, 12300:12300 + 2048]
                e.tensor_add(out=w, in0=y, in1=ud)
                e.tensor_mul(out=w, in0=w, in1=sr)
                hb = K.ba[:, par * 4096:par * 4096 + 2048]
                lb = K.ba[:, par * 4096 + 2048:par * 4096 + 4096]
                t2 = K.fa[:, 14400:14400 + 2048]
                e.tensor_copy(out=hb, in_=w)
                e.tensor_copy(out=t2, in_=hb)
                e.tensor_sub(out=t2, in0=w, in1=t2)
                vec.inc(e.tensor_copy(out=lb, in_=t2), 1)

        def a_gpsimd(e):
            for kc in range(16):
                par = kc % 2
                e.wait_ge(vec.h, B["vec"] + (kc + 1))
                gp.inc(e.dma_start(out=K.yfT_sp[0].ap()[kc],
                                   in_=K.ba[:, par * 4096:par * 4096 + 2048]), 16)
                gp.inc(e.dma_start(out=K.yfT_sp[1].ap()[kc],
                                   in_=K.ba[:, par * 4096 + 2048:par * 4096 + 4096]), 16)

        K.blk.sync(a_sync)
        K.blk.vector(a_vector)
        K.blk.gpsimd(a_gpsimd)

        if "yf" in K.dbg:
            Bx = self.snap()

            def t_gy(e):
                e.wait_ge(gp.h, Bx["gp"])
                for c2 in range(16):
                    S["st"].inc(e.dma_start(out=K.dbg["yf"].ap()[c2],
                                            in_=K.yfT_sp[1].ap()[c2]), 16)
                e.wait_ge(S["st"].h, S["st"].c)
            K.blk.gpsimd(t_gy)

    def ph_p3b(self):
        K = self
        S = K.S
        ld, te, vec, gp = (S[n] for n in ["ld", "te", "vec", "gp"])
        B = self.snap()

        def b_sync(e):
            e.wait_ge(vec.h, B["vec"])
            e.wait_ge(gp.h, B["gp"])
            for kc in range(16):
                ld.inc(e.dma_start(out=K.xnT_hs[:, kc * 1024:(kc + 1) * 1024],
                                   in_=K.wsp["wop"][0].ap()[kc * 128:(kc + 1) * 128, :]), 16)
                ld.inc(e.dma_start(out=K.xnT_ls[:, kc * 1024:(kc + 1) * 1024],
                                   in_=K.wsp["wop"][1].ap()[kc * 128:(kc + 1) * 128, :]), 16)
            e.wait_ge(ld.h, B["ld"] + 16 * 32)
            ld2 = S["ld2"]
            for mt in range(16):
                if mt >= 2:
                    e.wait_ge(te.h, B["te"] + 16 * (mt - 1))
                e.wait_ge(ld2.h, B["ld2"] + 32 * 16 * mt)
                for kc in range(16):
                    par = (mt % 2) * 16 + kc
                    ld2.inc(e.dma_start(
                        out=K.ba[:, par * 256:par * 256 + 128],
                        in_=K.yfT_sp[0].ap()[kc, :, mt * 128:(mt + 1) * 128]), 16)
                    ld2.inc(e.dma_start(
                        out=K.ba[:, par * 256 + 128:par * 256 + 256],
                        in_=K.yfT_sp[1].ap()[kc, :, mt * 128:(mt + 1) * 128]), 16)

        def b_tensor(e):
            e.wait_ge(ld.h, B["ld"] + 16 * 32)
            for mt in range(16):
                ps = K.ps_b if mt % 2 == 0 else K.ps_a
                if mt >= 2:
                    e.wait_ge(vec.h, B["vec"] + (mt - 1))
                if True:
                    e.wait_ge(S["ld2"].h, B["ld2"] + 32 * 16 * (mt + 1))
                for kc in range(16):
                    i = mt * 16 + kc
                    par = (mt % 2) * 16 + kc
                    yfh = K.ba[:, par * 256:par * 256 + 128]
                    yfl = K.ba[:, par * 256 + 128:par * 256 + 256]
                    wh = K.xnT_hs[:, kc * 1024:(kc + 1) * 1024]
                    wl = K.xnT_ls[:, kc * 1024:(kc + 1) * 1024]
                    e.ldweights(yfh)
                    for nq in range(2):
                        e.matmul(out=ps[:, nq * 512:(nq + 1) * 512], lhsT=yfh,
                                 rhs=wh[:, nq * 512:(nq + 1) * 512],
                                 start=(kc == 0), stop=False)
                    for nq in range(2):
                        e.matmul(out=ps[:, nq * 512:(nq + 1) * 512], lhsT=yfh,
                                 rhs=wl[:, nq * 512:(nq + 1) * 512],
                                 start=False, stop=False)
                    e.ldweights(yfl)
                    for nq in range(2):
                        ins = e.matmul(out=ps[:, nq * 512:(nq + 1) * 512], lhsT=yfl,
                                       rhs=wh[:, nq * 512:(nq + 1) * 512],
                                       start=False, stop=(kc == 15))
                    te.inc(ins, 1)

        def b_vector(e):
            for mt in range(16):
                par2 = mt % 2
                e.wait_ge(te.h, B["te"] + 16 * (mt + 1))
                if mt >= 2:
                    e.wait_ge(gp.h, B["gp"] + 16 * (mt - 1))
                ps = K.ps_b if par2 == 0 else K.ps_a
                vec.inc(e.tensor_copy(
                    out=K.fa[:, par2 * 1024:par2 * 1024 + 1024],
                    in_=ps[:, 0:1024]), 1)

        def b_gpsimd(e):
            for mt in range(16):
                par2 = mt % 2
                e.wait_ge(vec.h, B["vec"] + (mt + 1))
                gp.inc(e.dma_start(out=K.hid_hbm.ap()[mt, :, 0:1024],
                                   in_=K.fa[:, par2 * 1024:par2 * 1024 + 1024]), 16)

        K.blk.sync(b_sync)
        K.blk.tensor(b_tensor)
        K.blk.vector(b_vector)
        K.blk.gpsimd(b_gpsimd)

    def ph_p3c(self):
        K = self
        S = K.S
        ld, te, vec, sc, gp, fv = (S[n] for n in
                                   ["ld", "te", "vec", "sc", "gp", "fv"])
        MB, XG, HP, SCR, RW, CL2 = 0, 4100, 8200, 12300, 14400, 18600
        B = self.snap()
        GV = 11

        def vfence(e):
            t = fv.c
            fv.inc(e.engine_nop(), 1)
            e.wait_ge(fv.h, t + 1)

        def c_sync(e):
            e.wait_ge(vec.h, B["vec"])
            ld.inc(e.dma_start(out=K.fa[:, RW:RW + 1024],
                               in_=brow(K.rows_in.ap()[R_LN0G:R_LN0G + 1, :])), 16)
            ld.inc(e.dma_start(out=K.fa[:, RW + 1024:RW + 2048],
                               in_=brow(K.rows_in.ap()[R_LN0B:R_LN0B + 1, :])), 16)
            for g in range(4):
                if g >= 1:
                    e.wait_ge(vec.h, B["vec"] + (g - 1) * GV + 11)
                ld.inc(e.dma_start(
                    out=K.fa[:, XG:XG + 4096].rearrange("p (i d) -> p i d", i=4),
                    in_=chunk_cols(K.x_in, 4, 1024,
                                   off=(HALO + g * 512) * 1024)), 16)
                ld.inc(e.dma_start(
                    out=K.fa[:, MB:MB + 4096].rearrange("p (i d) -> p i d", i=4),
                    in_=AP(K.hid_hbm.ap().tensor, g * 4 * 128 * 2048,
                           [[2048, 128], [128 * 2048, 4], [1, 1024]])), 16)

        def c_vector(e):
            c = K.fa
            for g in range(4):
                e.wait_ge(ld.h, B["ld"] + 16 * (2 + 2 * (g + 1)))
                for i in range(4):
                    xt = c[:, XG + i * 1024:XG + (i + 1) * 1024]
                    e.tensor_reduce(out=c[:, CL2 + i:CL2 + i + 1], in_=xt,
                                    axis=AX.X, op=ALU.add)
                    e.tensor_mul(out=c[:, SCR:SCR + 1024], in0=xt, in1=xt)
                    e.tensor_reduce(out=c[:, CL2 + 8 + i:CL2 + 9 + i],
                                    in_=c[:, SCR:SCR + 1024], axis=AX.X, op=ALU.add)
                vfence(e)
                e.tensor_scalar(out=c[:, CL2:CL2 + 4], in0=c[:, CL2:CL2 + 4],
                                scalar1=1.0 / DM, scalar2=None, op0=ALU.mult)
                e.tensor_scalar(out=c[:, CL2 + 8:CL2 + 12], in0=c[:, CL2 + 8:CL2 + 12],
                                scalar1=1.0 / DM, scalar2=None, op0=ALU.mult)
                vfence(e)
                e.tensor_mul(out=c[:, CL2 + 16:CL2 + 20], in0=c[:, CL2:CL2 + 4],
                             in1=c[:, CL2:CL2 + 4])
                vfence(e)
                vec.inc(e.tensor_sub(out=c[:, CL2 + 16:CL2 + 20],
                                     in0=c[:, CL2 + 8:CL2 + 12],
                                     in1=c[:, CL2 + 16:CL2 + 20]), 1)
                e.wait_ge(sc.h, B["sc"] + 2 * g + 1)
                e.reciprocal(out=c[:, CL2 + 24:CL2 + 28], in_=c[:, CL2 + 32:CL2 + 36])
                vfence(e)
                e.tensor_mul(out=c[:, CL2 + 28:CL2 + 32], in0=c[:, CL2:CL2 + 4],
                             in1=c[:, CL2 + 24:CL2 + 28])
                vfence(e)
                if g >= 1:
                    e.wait_ge(gp.h, B["gp"] + 272 * 4 * g)
                for i in range(4):
                    xt = c[:, XG + i * 1024:XG + (i + 1) * 1024]
                    e.tensor_scalar(out=xt, in0=xt,
                                    scalar1=c[:, CL2 + 24 + i:CL2 + 25 + i],
                                    scalar2=c[:, CL2 + 28 + i:CL2 + 29 + i],
                                    op0=ALU.mult, op1=ALU.subtract)
                    e.tensor_add(out=c[:, HP + i * 1024:HP + (i + 1) * 1024],
                                 in0=xt, in1=c[:, MB + i * 1024:MB + (i + 1) * 1024])
                for i in range(4):
                    ht = c[:, HP + i * 1024:HP + (i + 1) * 1024]
                    e.tensor_reduce(out=c[:, CL2 + 40 + i:CL2 + 41 + i], in_=ht,
                                    axis=AX.X, op=ALU.add)
                    e.tensor_mul(out=c[:, SCR:SCR + 1024], in0=ht, in1=ht)
                    e.tensor_reduce(out=c[:, CL2 + 48 + i:CL2 + 49 + i],
                                    in_=c[:, SCR:SCR + 1024], axis=AX.X, op=ALU.add)
                vfence(e)
                e.tensor_scalar(out=c[:, CL2 + 40:CL2 + 44], in0=c[:, CL2 + 40:CL2 + 44],
                                scalar1=1.0 / DM, scalar2=None, op0=ALU.mult)
                e.tensor_scalar(out=c[:, CL2 + 48:CL2 + 52], in0=c[:, CL2 + 48:CL2 + 52],
                                scalar1=1.0 / DM, scalar2=None, op0=ALU.mult)
                vfence(e)
                e.tensor_mul(out=c[:, CL2 + 56:CL2 + 60], in0=c[:, CL2 + 40:CL2 + 44],
                             in1=c[:, CL2 + 40:CL2 + 44])
                vfence(e)
                vec.inc(e.tensor_sub(out=c[:, CL2 + 56:CL2 + 60],
                                     in0=c[:, CL2 + 48:CL2 + 52],
                                     in1=c[:, CL2 + 56:CL2 + 60]), 1)
                e.wait_ge(sc.h, B["sc"] + 2 * g + 2)
                e.reciprocal(out=c[:, CL2 + 64:CL2 + 68], in_=c[:, CL2 + 72:CL2 + 76])
                vfence(e)
                e.tensor_mul(out=c[:, CL2 + 68:CL2 + 72], in0=c[:, CL2 + 40:CL2 + 44],
                             in1=c[:, CL2 + 64:CL2 + 68])
                vfence(e)
                for i in range(4):
                    ht = c[:, HP + i * 1024:HP + (i + 1) * 1024]
                    e.tensor_scalar(out=ht, in0=ht,
                                    scalar1=c[:, CL2 + 64 + i:CL2 + 65 + i],
                                    scalar2=c[:, CL2 + 68 + i:CL2 + 69 + i],
                                    op0=ALU.mult, op1=ALU.subtract)
                    e.tensor_mul(out=ht, in0=ht, in1=c[:, RW:RW + 1024])
                    e.tensor_add(out=ht, in0=ht, in1=c[:, RW + 1024:RW + 2048])
                vec.inc(e.engine_nop(), 1)
                for i in range(4):
                    mtv = 4 * g + i
                    e.wait_ge(te.h, B["te"] + 8 * (4 * g + i + 1))
                    e.tensor_copy(out=c[:, SCR:SCR + 1024], in_=K.ps_a[:, 1024:2048])
                    vec.inc(e.engine_nop(), 1)
                    if mtv >= 2:
                        e.wait_ge(gp.h, B["gp"] + 272 * (mtv - 1))
                    hb = K.ba[:, (i % 2) * 2048:(i % 2) * 2048 + 1024]
                    lb = K.ba[:, (i % 2) * 2048 + 1024:(i % 2) * 2048 + 2048]
                    for kc in range(8):
                        e.tensor_copy(out=hb[:, kc * 128:(kc + 1) * 128],
                                      in_=c[:, SCR + kc * 128:SCR + (kc + 1) * 128])
                    for kc in range(8):
                        e.tensor_copy(out=c[:, MB + kc * 128:MB + (kc + 1) * 128],
                                      in_=hb[:, kc * 128:(kc + 1) * 128])
                    for kc in range(8):
                        e.tensor_sub(out=c[:, MB + kc * 128:MB + (kc + 1) * 128],
                                     in0=c[:, SCR + kc * 128:SCR + (kc + 1) * 128],
                                     in1=c[:, MB + kc * 128:MB + (kc + 1) * 128])
                    for kc in range(8):
                        e.tensor_copy(out=lb[:, kc * 128:(kc + 1) * 128],
                                      in_=c[:, MB + kc * 128:MB + (kc + 1) * 128])
                    vec.inc(e.engine_nop(), 1)

        def c_scalar(e):
            for g in range(4):
                e.wait_ge(vec.h, B["vec"] + g * GV + 1)
                sc.inc(e.activation(out=K.fa[:, CL2 + 32:CL2 + 36],
                                    in_=K.fa[:, CL2 + 16:CL2 + 20], func=AF.Sqrt,
                                    bias=K.consts[:, K.CC_EPS0:K.CC_EPS0 + 1]), 1)
                e.wait_ge(vec.h, B["vec"] + g * GV + 2)
                sc.inc(e.activation(out=K.fa[:, CL2 + 72:CL2 + 76],
                                    in_=K.fa[:, CL2 + 56:CL2 + 60], func=AF.Sqrt,
                                    bias=K.consts[:, K.CC_EPSA:K.CC_EPSA + 1]), 1)

        def c_tensor(e):
            for g in range(4):
                e.wait_ge(vec.h, B["vec"] + g * GV + 3)
                for i in range(4):
                    if g + i > 0:
                        e.wait_ge(vec.h, B["vec"] + g * GV + 3 + 2 * i)
                    for kc in range(8):
                        te.inc(e.transpose(
                            out=K.ps_a[:, 1024 + kc * 128:1024 + (kc + 1) * 128],
                            in_=K.fa[:, HP + i * 1024 + kc * 128:
                                     HP + i * 1024 + (kc + 1) * 128],
                            identity=K.ident_s[:]), 1)

        def c_gpsimd(e):
            for g in range(4):
                for i in range(4):
                    mt = g * 4 + i
                    e.wait_ge(vec.h, B["vec"] + g * GV + 3 + 2 * (i + 1))
                    e.wait_ge(gp.h, B["gp"] + 272 * mt)
                    hb = K.ba[:, (i % 2) * 2048:(i % 2) * 2048 + 1024]
                    lb = K.ba[:, (i % 2) * 2048 + 1024:(i % 2) * 2048 + 2048]
                    gp.inc(e.dma_start(out=K.h_hbm.ap()[mt],
                                       in_=K.fa[:, HP + i * 1024:HP + (i + 1) * 1024]), 16)
                    for kc in range(8):
                        gp.inc(e.dma_start(
                            out=K.hT_sp[0].ap()[kc, :, mt * 128:(mt + 1) * 128],
                            in_=hb[:, kc * 128:(kc + 1) * 128]), 16)
                        gp.inc(e.dma_start(
                            out=K.hT_sp[1].ap()[kc, :, mt * 128:(mt + 1) * 128],
                            in_=lb[:, kc * 128:(kc + 1) * 128]), 16)

        K.blk.sync(c_sync)
        K.blk.vector(c_vector)
        K.blk.scalar(c_scalar)
        K.blk.tensor(c_tensor)
        K.blk.gpsimd(c_gpsimd)

        if "m" in K.dbg:
            Bm = self.snap()

            def t_gm(e):
                e.wait_ge(gp.h, Bm["gp"])
                for c2 in range(16):
                    S["st"].inc(e.dma_start(out=K.dbg["m"].ap()[c2],
                                            in_=K.hid_hbm.ap()[c2, :, 0:1024]), 16)
                e.wait_ge(S["st"].h, S["st"].c)
            K.blk.gpsimd(t_gm)

        if "h" in K.dbg:
            Bx = self.snap()

            def t_gp(e):
                e.wait_ge(gp.h, Bx["gp"])
                for c2 in range(16):
                    S["st"].inc(e.dma_start(out=K.dbg["h"].ap()[c2],
                                            in_=K.h_hbm.ap()[c2]), 16)
                e.wait_ge(S["st"].h, S["st"].c)
            K.blk.gpsimd(t_gp)

    def ph_p3d(self):
        K = self
        S = K.S
        ld, ld2, te, vec, sc, gp, cc, fv = (S[n] for n in
                                            ["ld", "ld2", "te", "vec", "sc", "gp",
                                             "cc", "fv"])
        QP, SCR, RW, CL2 = 0, 12300, 14400, 18600

        def vfence(e):
            t = fv.c
            fv.inc(e.engine_nop(), 1)
            e.wait_ge(fv.h, t + 1)

        for phase in ("q", "k"):
            B = self.snap()
            wsp = K.wsp["wq" if phase == "q" else "wk"]
            bias_r = R_QB if phase == "q" else R_KB
            VI = 5 if phase == "q" else 4
            TEI = 10 if phase == "q" else 8
            SI = 2 if phase == "q" else 1
            NLD = 18 if phase == "q" else 17

            def d_sync(e, wsp=wsp, bias_r=bias_r, B=B, phase=phase):
                e.wait_ge(vec.h, B["vec"])
                e.wait_ge(te.h, B["te"])
                e.wait_ge(gp.h, B["gp"])
                for kc in range(8):
                    ld.inc(e.dma_start(out=K.xnT_hs[:, kc * 1024:(kc + 1) * 1024],
                                       in_=wsp[0].ap()[kc * 128:(kc + 1) * 128, :]), 16)
                    ld.inc(e.dma_start(out=K.xnT_ls[:, kc * 1024:(kc + 1) * 1024],
                                       in_=wsp[1].ap()[kc * 128:(kc + 1) * 128, :]), 16)
                ld.inc(e.dma_start(out=K.fa[:, RW:RW + 1024],
                                   in_=brow(K.rows_in.ap()[bias_r:bias_r + 1, :])), 16)
                if phase == "q":
                    ld.inc(e.dma_start(out=K.fa[:, RW + 1024:RW + 2048],
                                       in_=brow(K.rows_in.ap()[R_WG:R_WG + 1, :])), 16)
                e.wait_ge(ld.h, B["ld"] + 16 * (17 + (1 if phase == "q" else 0)))
                for mt in range(16):
                    if mt >= 2:
                        e.wait_ge(te.h, B["te"] + TEI * (mt - 1))
                    e.wait_ge(ld2.h, B["ld2"] + 16 * 16 * mt)
                    for kc in range(8):
                        sl = (mt % 2) * 8 + kc
                        ld2.inc(e.dma_start(
                            out=K.ba[:, sl * 128:(sl + 1) * 128],
                            in_=K.hT_sp[0].ap()[kc, :, mt * 128:(mt + 1) * 128]), 16)
                        ld2.inc(e.dma_start(
                            out=K.ba[:, 2048 + sl * 128:2048 + (sl + 1) * 128],
                            in_=K.hT_sp[1].ap()[kc, :, mt * 128:(mt + 1) * 128]), 16)

            def d_tensor(e, B=B, phase=phase, VI=VI, TEI=TEI):
                for mt in range(16):
                    ps = K.ps_a if mt % 2 == 0 else K.ps_b
                    if mt >= 2:
                        e.wait_ge(vec.h, B["vec"] + VI * (mt - 2) + 1)
                    e.wait_ge(ld2.h, B["ld2"] + 16 * 16 * (mt + 1))
                    for kc in range(8):
                        sl = (mt % 2) * 8 + kc
                        th = K.ba[:, sl * 128:(sl + 1) * 128]
                        tl = K.ba[:, 2048 + sl * 128:2048 + (sl + 1) * 128]
                        wh = K.xnT_hs[:, kc * 1024:(kc + 1) * 1024]
                        wl = K.xnT_ls[:, kc * 1024:(kc + 1) * 1024]
                        e.ldweights(th)
                        for nq in range(2):
                            e.matmul(out=ps[:, nq * 512:(nq + 1) * 512], lhsT=th,
                                     rhs=wh[:, nq * 512:(nq + 1) * 512],
                                     start=(kc == 0), stop=False)
                        for nq in range(2):
                            e.matmul(out=ps[:, nq * 512:(nq + 1) * 512], lhsT=th,
                                     rhs=wl[:, nq * 512:(nq + 1) * 512],
                                     start=False, stop=False)
                        e.ldweights(tl)
                        for nq in range(2):
                            ins = e.matmul(out=ps[:, nq * 512:(nq + 1) * 512], lhsT=tl,
                                           rhs=wh[:, nq * 512:(nq + 1) * 512],
                                           start=False, stop=(kc == 7))
                        te.inc(ins, 1)
                    if phase == "q":
                        e.wait_ge(vec.h, B["vec"] + VI * mt + 4)
                        te.inc(e.matmul(out=K.ps_a[0:1, 1024:1536],
                                        lhsT=K.fa[:, CL2 + 30:CL2 + 31],
                                        rhs=K.fa[:, QP:QP + 512],
                                        start=(mt == 0), stop=(mt == 15)), 1)
                        te.inc(e.matmul(out=K.ps_a[0:1, 1536:2048],
                                        lhsT=K.fa[:, CL2 + 30:CL2 + 31],
                                        rhs=K.fa[:, QP + 512:QP + 1024],
                                        start=(mt == 0), stop=(mt == 15)), 1)

            def d_vector(e, B=B, phase=phase, VI=VI, TEI=TEI, SI=SI):
                for mt in range(16):
                    ps = K.ps_a if mt % 2 == 0 else K.ps_b
                    e.wait_ge(te.h, B["te"] + TEI * mt + 8)
                    if phase == "k" and mt >= 1:
                        e.wait_ge(gp.h, B["gp"] + 16 * mt)
                    c = K.fa
                    e.tensor_copy(out=c[:, QP:QP + 1024], in_=ps[:, 0:1024])
                    vec.inc(e.tensor_add(out=c[:, QP:QP + 1024], in0=c[:, QP:QP + 1024],
                                         in1=c[:, RW:RW + 1024]), 1)
                    e.tensor_mul(out=c[:, SCR:SCR + 1024], in0=c[:, QP:QP + 1024],
                                 in1=c[:, QP:QP + 1024])
                    vec.inc(e.tensor_reduce(out=c[:, CL2 + 20:CL2 + 21],
                                            in_=c[:, SCR:SCR + 1024],
                                            axis=AX.X, op=ALU.add), 1)
                    e.wait_ge(sc.h, B["sc"] + SI * mt + 1)
                    e.tensor_scalar(out=c[:, CL2 + 22:CL2 + 23],
                                    in0=c[:, CL2 + 21:CL2 + 22],
                                    scalar1=1e-12, scalar2=None, op0=ALU.max)
                    vfence(e)
                    e.reciprocal(out=c[:, CL2 + 23:CL2 + 24], in_=c[:, CL2 + 22:CL2 + 23])
                    vfence(e)
                    vec.inc(e.tensor_scalar(out=c[:, QP:QP + 1024], in0=c[:, QP:QP + 1024],
                                            scalar1=c[:, CL2 + 23:CL2 + 24],
                                            scalar2=None, op0=ALU.mult), 1)
                    if phase == "q":
                        e.wait_ge(sc.h, B["sc"] + 2 * mt + 2)
                        e.tensor_mul(out=c[:, SCR:SCR + 1024], in0=c[:, QP:QP + 1024],
                                     in1=c[:, RW + 1024:RW + 2048])
                        e.tensor_reduce(out=c[:, CL2 + 28:CL2 + 29],
                                        in_=c[:, SCR:SCR + 1024], axis=AX.X, op=ALU.add)
                        vfence(e)
                        e.tensor_scalar(out=c[:, CL2 + 30:CL2 + 31],
                                        in0=c[:, CL2 + 28:CL2 + 29],
                                        scalar1=float(DM) ** -0.5, scalar2=None,
                                        op0=ALU.mult)
                        vfence(e)
                        e.tensor_mul(out=c[:, CL2 + mt:CL2 + mt + 1],
                                     in0=c[:, CL2 + 30:CL2 + 31],
                                     in1=c[:, CL2 + 30:CL2 + 31])
                        vec.inc(e.engine_nop(), 1)
                        e.wait_ge(te.h, B["te"] + TEI * mt + 10)
                        vec.inc(e.engine_nop(), 1)
                    else:
                        vec.inc(e.engine_nop(), 1)

            def d_scalar(e, B=B, phase=phase, VI=VI):
                for mt in range(16):
                    e.wait_ge(vec.h, B["vec"] + VI * mt + 2)
                    sc.inc(e.activation(out=K.fa[:, CL2 + 21:CL2 + 22],
                                        in_=K.fa[:, CL2 + 20:CL2 + 21],
                                        func=AF.Sqrt), 1)
                    if phase == "q":
                        e.wait_ge(vec.h, B["vec"] + VI * mt + 3)
                        sc.inc(e.activation(out=K.fa[:, QP:QP + 1024],
                                            in_=K.fa[:, QP:QP + 1024],
                                            func=AF.Gelu), 1)

            def d_gpsimd(e, B=B):
                for mt in range(16):
                    e.wait_ge(vec.h, B["vec"] + 4 * mt + 4)
                    gp.inc(e.dma_start(out=K.k_hbm.ap()[mt],
                                       in_=K.fa[:, QP:QP + 1024]), 16)

            K.blk.sync(d_sync)
            K.blk.tensor(d_tensor)
            K.blk.vector(d_vector)
            K.blk.scalar(d_scalar)
            if phase == "k":
                K.blk.gpsimd(d_gpsimd)
                continue

            B2 = self.snap()

            def g_vector(e):
                e.wait_ge(vec.h, B2["vec"])
                c = K.fa
                vfence(e)
                vec.inc(e.tensor_reduce(out=c[:, CL2 + 32:CL2 + 33],
                                        in_=c[:, CL2:CL2 + 16],
                                        axis=AX.X, op=ALU.add), 1)
                e.wait_ge(te.h, B2["te"] + 1)
                e.tensor_copy(out=c[0:1, SCR:SCR + 1024], in_=K.ps_a[0:1, 1024:2048])
                e.tensor_copy(out=c[0:1, SCR + 1024:SCR + 1025], in_=K.ps_b[0:1, 0:1])
                vec.inc(e.engine_nop(), 1)
                e.wait_ge(te.h, B2["te"] + 4)
                vec.inc(e.tensor_copy(out=c[0:4, SCR + 2048:SCR + 3080],
                                      in_=K.ps_b[0:4, 0:1032]), 1)

            def g_tensor(e):
                e.wait_ge(vec.h, B2["vec"] + 1)
                te.inc(e.matmul(out=K.ps_b[0:1, 0:1],
                                lhsT=K.consts[:, K.CC_ONE:K.CC_ONE + 1],
                                rhs=K.fa[:, CL2 + 32:CL2 + 33],
                                start=True, stop=True), 1)
                e.wait_ge(vec.h, B2["vec"] + 2)
                for i, w in ((0, 512), (1, 512), (2, 8)):
                    te.inc(e.matmul(out=K.ps_b[0:4, i * 512:i * 512 + w],
                                    lhsT=K.oh_r[0:1, :],
                                    rhs=K.fa[0:1, SCR + i * 512:SCR + i * 512 + w],
                                    start=True, stop=True), 1)

            def g_gpsimd(e):
                e.wait_ge(vec.h, B2["vec"] + 3)
                gpt = gp.c
                gp.inc(e.dma_start(out=K.gar_in.ap(),
                                   in_=K.fa[0:4, SCR + 2048:SCR + 3080]), 16)
                e.wait_ge(gp.h, gpt + 16)
                cc.inc(e.collective_compute(
                    "AllReduce", ALU.add, replica_groups=[list(range(NCORE))],
                    ins=[K.gar_in.ap().opt()], outs=[K.gar_out.ap().opt()]), 1)
                e.wait_ge(cc.h, cc.c)
                gp.inc(e.dma_start(out=K.fa[0:4, SCR + 2048:SCR + 3080],
                                   in_=K.gar_out.ap()), 16)

            def g_tensor2(e):
                e.wait_ge(S["gp"].h, B2["gp"] + 32)
                for i in range(2):
                    te.inc(e.matmul(out=K.ps_a[0:1, 1024 + i * 512:1024 + (i + 1) * 512],
                                    lhsT=K.oh_c[:, 0:1],
                                    rhs=K.fa[0:4, SCR + 2048 + i * 512:
                                             SCR + 2048 + (i + 1) * 512],
                                    start=True, stop=True), 1)
                te.inc(e.matmul(out=K.ps_b[0:1, 1024:1032],
                                lhsT=K.oh_c[:, 0:1],
                                rhs=K.fa[0:4, SCR + 2048 + 1024:SCR + 2048 + 1032],
                                start=True, stop=True), 1)

            def g_vector2(e):
                e.wait_ge(te.h, B2["te"] + 7)
                c = K.fa
                e.tensor_copy(out=c[0:1, SCR:SCR + 1024], in_=K.ps_a[0:1, 1024:2048])
                vec.inc(e.tensor_copy(out=c[0:1, CL2 + 34:CL2 + 35],
                                      in_=K.ps_b[0:1, 1024:1025]), 1)

            K.blk.vector(g_vector)
            K.blk.tensor(g_tensor)
            K.blk.gpsimd(g_gpsimd)
            K.blk.tensor(g_tensor2)
            K.blk.vector(g_vector2)

            B3 = self.snap()

            def g_scalar3(e):
                e.wait_ge(vec.h, B3["vec"])
                sc.inc(e.activation(out=K.fa[0:1, CL2 + 35:CL2 + 36],
                                    in_=K.fa[0:1, CL2 + 34:CL2 + 35],
                                    func=AF.Sqrt), 1)

            def g_vector3(e):
                e.wait_ge(sc.h, B3["sc"] + 1)
                c = K.fa
                e.tensor_scalar(out=c[0:1, CL2 + 36:CL2 + 37],
                                in0=c[0:1, CL2 + 35:CL2 + 36],
                                scalar1=1e-12, scalar2=None, op0=ALU.max)
                vfence(e)
                e.reciprocal(out=c[0:1, CL2 + 37:CL2 + 38], in_=c[0:1, CL2 + 36:CL2 + 37])
                vfence(e)
                vec.inc(e.tensor_scalar(out=c[0:1, SCR:SCR + 1024],
                                        in0=c[0:1, SCR:SCR + 1024],
                                        scalar1=c[0:1, CL2 + 37:CL2 + 38],
                                        scalar2=None, op0=ALU.mult), 1)

            def g_gpsimd3(e):
                e.wait_ge(vec.h, B3["vec"] + 1)
                gp.inc(e.dma_start(out=K.gar_in.ap()[0:1, 0:1024],
                                   in_=K.fa[0:1, SCR:SCR + 1024]), 16)

            K.blk.scalar(g_scalar3)
            K.blk.vector(g_vector3)
            K.blk.gpsimd(g_gpsimd3)

        if "k" in K.dbg:
            Bk = self.snap()

            def t_gk(e):
                e.wait_ge(gp.h, Bk["gp"])
                for c2 in range(16):
                    S["st"].inc(e.dma_start(out=K.dbg["k"].ap()[c2],
                                            in_=K.k_hbm.ap()[c2]), 16)
                e.wait_ge(S["st"].h, S["st"].c)
            K.blk.gpsimd(t_gk)

        if "G" in K.dbg:
            BG = self.snap()

            def t_gG(e):
                e.wait_ge(gp.h, BG["gp"])
                S["st"].inc(e.dma_start(out=K.dbg["G"].ap(),
                                        in_=K.gar_in.ap()[0:1, 0:1024]), 16)
                e.wait_ge(S["st"].h, S["st"].c)
            K.blk.gpsimd(t_gG)

    def ph_p3f1(self):
        K = self
        S = K.S
        ld, te, vec, gp, fv = (S[n] for n in ["ld", "te", "vec", "gp", "fv"])
        HP, SCR, RW = 8200, 12300, 14400
        B = self.snap()

        def f_sync(e):
            e.wait_ge(vec.h, B["vec"])
            e.wait_ge(gp.h, B["gp"])
            ld.inc(e.dma_start(out=K.fa[:, RW:RW + 1024],
                               in_=brow(K.gar_in.ap()[0:1, 0:1024])), 16)
            e.wait_ge(ld.h, B["ld"] + 16)
            for mt in range(16):
                if mt >= 2:
                    e.wait_ge(vec.h, B["vec"] + 3 * (mt - 1))
                e.wait_ge(ld.h, B["ld"] + 16 * (mt + 1))
                ld.inc(e.dma_start(out=K.fa[:, (mt % 2) * 1024:(mt % 2) * 1024 + 1024],
                                   in_=K.k_hbm.ap()[mt]), 16)

        def f_vector(e):
            for mt in range(16):
                par = mt % 2
                e.wait_ge(ld.h, B["ld"] + 16 * (mt + 2))
                vec.inc(e.tensor_mul(out=K.fa[:, HP:HP + 1024],
                                     in0=K.fa[:, par * 1024:par * 1024 + 1024],
                                     in1=K.fa[:, RW:RW + 1024]), 1)
                e.wait_ge(te.h, B["te"] + 8 * (mt + 1))
                e.tensor_copy(out=K.fa[:, SCR:SCR + 1024], in_=K.ps_a[:, 1024:2048])
                vec.inc(e.engine_nop(), 1)
                if mt >= 2:
                    e.wait_ge(gp.h, B["gp"] + 16 * 16 * (mt - 1))
                hb = K.ba[:, (mt % 2) * 2048:(mt % 2) * 2048 + 1024]
                lb = K.ba[:, (mt % 2) * 2048 + 1024:(mt % 2) * 2048 + 2048]
                c = K.fa
                for kc in range(8):
                    e.tensor_copy(out=hb[:, kc * 128:(kc + 1) * 128],
                                  in_=c[:, SCR + kc * 128:SCR + (kc + 1) * 128])
                for kc in range(8):
                    e.tensor_copy(out=c[:, 4100 + kc * 128:4100 + (kc + 1) * 128],
                                  in_=hb[:, kc * 128:(kc + 1) * 128])
                for kc in range(8):
                    e.tensor_sub(out=c[:, 4100 + kc * 128:4100 + (kc + 1) * 128],
                                 in0=c[:, SCR + kc * 128:SCR + (kc + 1) * 128],
                                 in1=c[:, 4100 + kc * 128:4100 + (kc + 1) * 128])
                for kc in range(8):
                    e.tensor_copy(out=lb[:, kc * 128:(kc + 1) * 128],
                                  in_=c[:, 4100 + kc * 128:4100 + (kc + 1) * 128])
                vec.inc(e.engine_nop(), 1)

        def f_tensor(e):
            for mt in range(16):
                e.wait_ge(vec.h, B["vec"] + 3 * mt + 1)
                if mt >= 1:
                    e.wait_ge(vec.h, B["vec"] + 3 * (mt - 1) + 2)
                for kc in range(8):
                    te.inc(e.transpose(
                        out=K.ps_a[:, 1024 + kc * 128:1024 + (kc + 1) * 128],
                        in_=K.fa[:, HP + kc * 128:HP + (kc + 1) * 128],
                        identity=K.ident_s[:]), 1)

        def f_gpsimd(e):
            for mt in range(16):
                e.wait_ge(vec.h, B["vec"] + 3 * mt + 3)
                e.wait_ge(gp.h, B["gp"] + 256 * mt)
                hb = K.ba[:, (mt % 2) * 2048:(mt % 2) * 2048 + 1024]
                lb = K.ba[:, (mt % 2) * 2048 + 1024:(mt % 2) * 2048 + 2048]
                for kc in range(8):
                    gp.inc(e.dma_start(
                        out=K.gkT_sp[0].ap()[kc, :, mt * 128:(mt + 1) * 128],
                        in_=hb[:, kc * 128:(kc + 1) * 128]), 16)
                    gp.inc(e.dma_start(
                        out=K.gkT_sp[1].ap()[kc, :, mt * 128:(mt + 1) * 128],
                        in_=lb[:, kc * 128:(kc + 1) * 128]), 16)

        K.blk.sync(f_sync)
        K.blk.vector(f_vector)
        K.blk.tensor(f_tensor)
        K.blk.gpsimd(f_gpsimd)

        if "gkT" in K.dbg:
            Bg = self.snap()

            def t_gkT(e):
                e.wait_ge(gp.h, Bg["gp"])
                for kc in range(8):
                    S["st"].inc(e.dma_start(
                        out=K.dbg["gkT"].ap()[kc],
                        in_=K.gkT_sp[0].ap()[kc]), 16)
                    S["st"].inc(e.dma_start(
                        out=K.dbg["gkTl"].ap()[kc],
                        in_=K.gkT_sp[1].ap()[kc]), 16)
                e.wait_ge(S["st"].h, S["st"].c)
            K.blk.gpsimd(t_gkT)

    def ph_mm_ln(self, wname, in_sp, bias_r, g_r, b_r, mode, out_sp,
                 store_h2=False):
        K = self
        S = K.S
        ld, ld2, te, vec, sc, gp, fv = (S[n] for n in
                                        ["ld", "ld2", "te", "vec", "sc", "gp", "fv"])
        QP, HREF, HP, SCR, RW, CL2 = 0, 4100, 8200, 12300, 14400, 18600
        B = self.snap()
        wsp = K.wsp[wname]
        VI = 6

        def vfence(e):
            t = fv.c
            fv.inc(e.engine_nop(), 1)
            e.wait_ge(fv.h, t + 1)

        def m_sync(e):
            e.wait_ge(vec.h, B["vec"])
            e.wait_ge(gp.h, B["gp"])
            e.wait_ge(te.h, B["te"])
            for kc in range(8):
                ld.inc(e.dma_start(out=K.xnT_hs[:, kc * 1024:(kc + 1) * 1024],
                                   in_=wsp[0].ap()[kc * 128:(kc + 1) * 128, :]), 16)
                ld.inc(e.dma_start(out=K.xnT_ls[:, kc * 1024:(kc + 1) * 1024],
                                   in_=wsp[1].ap()[kc * 128:(kc + 1) * 128, :]), 16)
            for rr, off in ((bias_r, 0), (g_r, 1024), (b_r, 2048)):
                ld.inc(e.dma_start(out=K.fa[:, RW + off:RW + off + 1024],
                                   in_=brow(K.rows_in.ap()[rr:rr + 1, :])), 16)
            e.wait_ge(ld.h, B["ld"] + 16 * 19)
            for mt in range(16):
                if mt >= 2:
                    e.wait_ge(vec.h, B["vec"] + VI * (mt - 1))
                e.wait_ge(ld.h, B["ld"] + 16 * (19 + mt))
                ld.inc(e.dma_start(out=K.fa[:, HREF + (mt % 2) * 1024:
                                            HREF + (mt % 2) * 1024 + 1024],
                                   in_=K.h_hbm.ap()[mt]), 16)
                if mt >= 2:
                    e.wait_ge(te.h, B["te"] + 16 * (mt - 1))
                e.wait_ge(ld2.h, B["ld2"] + 16 * 16 * mt)
                for kc in range(8):
                    sl = (mt % 2) * 8 + kc
                    ld2.inc(e.dma_start(
                        out=K.ba[:, sl * 128:(sl + 1) * 128],
                        in_=in_sp[0].ap()[kc, :, mt * 128:(mt + 1) * 128]), 16)
                    ld2.inc(e.dma_start(
                        out=K.ba[:, 2048 + sl * 128:2048 + (sl + 1) * 128],
                        in_=in_sp[1].ap()[kc, :, mt * 128:(mt + 1) * 128]), 16)

        def m_tensor(e):
            e.wait_ge(ld.h, B["ld"] + 16 * 19)
            for mt in range(16):
                ps = K.ps_a if mt % 2 == 0 else K.ps_b
                if mt >= 2:
                    e.wait_ge(vec.h, B["vec"] + VI * (mt - 2) + 1)
                e.wait_ge(ld2.h, B["ld2"] + 16 * 16 * (mt + 1))
                for kc in range(8):
                    sl = (mt % 2) * 8 + kc
                    th = K.ba[:, sl * 128:(sl + 1) * 128]
                    tl = K.ba[:, 2048 + sl * 128:2048 + (sl + 1) * 128]
                    wh = K.xnT_hs[:, kc * 1024:(kc + 1) * 1024]
                    wl = K.xnT_ls[:, kc * 1024:(kc + 1) * 1024]
                    e.ldweights(th)
                    for nq in range(2):
                        e.matmul(out=ps[:, nq * 512:(nq + 1) * 512], lhsT=th,
                                 rhs=wh[:, nq * 512:(nq + 1) * 512],
                                 start=(kc == 0), stop=False)
                    for nq in range(2):
                        e.matmul(out=ps[:, nq * 512:(nq + 1) * 512], lhsT=th,
                                 rhs=wl[:, nq * 512:(nq + 1) * 512],
                                 start=False, stop=False)
                    e.ldweights(tl)
                    for nq in range(2):
                        ins = e.matmul(out=ps[:, nq * 512:(nq + 1) * 512], lhsT=tl,
                                       rhs=wh[:, nq * 512:(nq + 1) * 512],
                                       start=False, stop=(kc == 7))
                    te.inc(ins, 1)
                e.wait_ge(vec.h, B["vec"] + VI * mt + 3)
                for kc in range(8):
                    te.inc(e.transpose(
                        out=K.ps_b[:, 1024 + kc * 128:1024 + (kc + 1) * 128],
                        in_=K.fa[:, HP + kc * 128:HP + (kc + 1) * 128],
                        identity=K.ident_s[:]), 1)

        def m_vector(e):
            c = K.fa
            for mt in range(16):
                par = mt % 2
                ps = K.ps_a if par == 0 else K.ps_b
                e.wait_ge(te.h, B["te"] + 16 * mt + 8)
                e.tensor_copy(out=c[:, QP:QP + 1024], in_=ps[:, 0:1024])
                vec.inc(e.tensor_add(out=c[:, QP:QP + 1024], in0=c[:, QP:QP + 1024],
                                     in1=c[:, RW:RW + 1024]), 1)
                e.wait_ge(ld.h, B["ld"] + 16 * (19 + mt + 1))
                href = c[:, HREF + par * 1024:HREF + par * 1024 + 1024]
                if mode == "add":
                    e.tensor_add(out=c[:, QP:QP + 1024], in0=c[:, QP:QP + 1024],
                                 in1=href)
                    lnin = c[:, QP:QP + 1024]
                else:
                    lnin = href
                e.tensor_reduce(out=c[:, CL2:CL2 + 1], in_=lnin, axis=AX.X, op=ALU.add)
                e.tensor_mul(out=c[:, SCR:SCR + 1024], in0=lnin, in1=lnin)
                e.tensor_reduce(out=c[:, CL2 + 1:CL2 + 2], in_=c[:, SCR:SCR + 1024],
                                axis=AX.X, op=ALU.add)
                vfence(e)
                e.tensor_scalar(out=c[:, CL2:CL2 + 2], in0=c[:, CL2:CL2 + 2],
                                scalar1=1.0 / DM, scalar2=None, op0=ALU.mult)
                vfence(e)
                e.tensor_mul(out=c[:, CL2 + 2:CL2 + 3], in0=c[:, CL2:CL2 + 1],
                             in1=c[:, CL2:CL2 + 1])
                vfence(e)
                vec.inc(e.tensor_sub(out=c[:, CL2 + 2:CL2 + 3],
                                     in0=c[:, CL2 + 1:CL2 + 2],
                                     in1=c[:, CL2 + 2:CL2 + 3]), 1)
                e.wait_ge(sc.h, B["sc"] + mt + 1)
                e.reciprocal(out=c[:, CL2 + 4:CL2 + 5], in_=c[:, CL2 + 3:CL2 + 4])
                vfence(e)
                e.tensor_mul(out=c[:, CL2 + 5:CL2 + 6], in0=c[:, CL2:CL2 + 1],
                             in1=c[:, CL2 + 4:CL2 + 5])
                vfence(e)
                e.tensor_scalar(out=c[:, SCR:SCR + 1024], in0=lnin,
                                scalar1=c[:, CL2 + 4:CL2 + 5],
                                scalar2=c[:, CL2 + 5:CL2 + 6],
                                op0=ALU.mult, op1=ALU.subtract)
                e.tensor_mul(out=c[:, SCR:SCR + 1024], in0=c[:, SCR:SCR + 1024],
                             in1=c[:, RW + 1024:RW + 2048])
                e.tensor_add(out=c[:, SCR:SCR + 1024], in0=c[:, SCR:SCR + 1024],
                             in1=c[:, RW + 2048:RW + 3072])
                if mode == "mul":
                    e.tensor_mul(out=c[:, HP:HP + 1024], in0=c[:, QP:QP + 1024],
                                 in1=c[:, SCR:SCR + 1024])
                else:
                    e.tensor_copy(out=c[:, HP:HP + 1024], in_=c[:, SCR:SCR + 1024])
                vec.inc(e.engine_nop(), 1)
                e.wait_ge(te.h, B["te"] + 16 * (mt + 1))
                e.tensor_copy(out=c[:, SCR:SCR + 1024], in_=K.ps_b[:, 1024:2048])
                vec.inc(e.engine_nop(), 1)
                if mt >= 2:
                    e.wait_ge(gp.h, B["gp"] + (17 if store_h2 else 16) * 16 * (mt - 1))
                hbb = K.ba[:, 4096 + (mt % 2) * 2048:4096 + (mt % 2) * 2048 + 1024]
                lbb = K.ba[:, 4096 + (mt % 2) * 2048 + 1024:
                           4096 + (mt % 2) * 2048 + 2048]
                for kc in range(8):
                    e.tensor_copy(out=hbb[:, kc * 128:(kc + 1) * 128],
                                  in_=c[:, SCR + kc * 128:SCR + (kc + 1) * 128])
                for kc in range(8):
                    e.tensor_copy(out=c[:, 4100 + kc * 128:4100 + (kc + 1) * 128]
                                  if False else c[:, QP + kc * 128:QP + (kc + 1) * 128],
                                  in_=hbb[:, kc * 128:(kc + 1) * 128])
                for kc in range(8):
                    e.tensor_sub(out=c[:, QP + kc * 128:QP + (kc + 1) * 128],
                                 in0=c[:, SCR + kc * 128:SCR + (kc + 1) * 128],
                                 in1=c[:, QP + kc * 128:QP + (kc + 1) * 128])
                for kc in range(8):
                    e.tensor_copy(out=lbb[:, kc * 128:(kc + 1) * 128],
                                  in_=c[:, QP + kc * 128:QP + (kc + 1) * 128])
                vec.inc(e.engine_nop(), 1)
                vec.inc(e.engine_nop(), 1)

        def m_scalar(e):
            for mt in range(16):
                e.wait_ge(vec.h, B["vec"] + VI * mt + 2)
                sc.inc(e.activation(out=K.fa[:, CL2 + 3:CL2 + 4],
                                    in_=K.fa[:, CL2 + 2:CL2 + 3], func=AF.Sqrt,
                                    bias=K.consts[:, K.CC_EPSA:K.CC_EPSA + 1]), 1)

        def m_gpsimd(e):
            for mt in range(16):
                e.wait_ge(vec.h, B["vec"] + VI * mt + 5)
                e.wait_ge(gp.h, B["gp"] + (17 if store_h2 else 16) * 16 * mt)
                hbb = K.ba[:, 4096 + (mt % 2) * 2048:4096 + (mt % 2) * 2048 + 1024]
                lbb = K.ba[:, 4096 + (mt % 2) * 2048 + 1024:
                           4096 + (mt % 2) * 2048 + 2048]
                if store_h2:
                    gp.inc(e.dma_start(out=K.h2_hbm.ap()[mt],
                                       in_=K.fa[:, HP:HP + 1024]), 16)
                for kc in range(8):
                    gp.inc(e.dma_start(
                        out=out_sp[0].ap()[kc, :, mt * 128:(mt + 1) * 128],
                        in_=hbb[:, kc * 128:(kc + 1) * 128]), 16)
                    gp.inc(e.dma_start(
                        out=out_sp[1].ap()[kc, :, mt * 128:(mt + 1) * 128],
                        in_=lbb[:, kc * 128:(kc + 1) * 128]), 16)

        K.blk.sync(m_sync)
        K.blk.tensor(m_tensor)
        K.blk.vector(m_vector)
        K.blk.scalar(m_scalar)
        K.blk.gpsimd(m_gpsimd)

        if store_h2 and "h2" in K.dbg:
            Bh2 = self.snap()

            def t_h2(e):
                e.wait_ge(gp.h, Bh2["gp"])
                for c2 in range(16):
                    S["st"].inc(e.dma_start(out=K.dbg["h2"].ap()[c2],
                                            in_=K.h2_hbm.ap()[c2]), 16)
                e.wait_ge(S["st"].h, S["st"].c)
            K.blk.gpsimd(t_h2)

    def ph_p3h(self):
        K = self
        S = K.S
        ld, ld2, ld3, te, vec, sc, gp, fv = (S[n] for n in
                                             ["ld", "ld2", "ld3", "te", "vec",
                                              "sc", "gp", "fv"])
        QP, HREF, XT, GT, SCR, RW, SCR2, CL2 = \
            0, 4100, 5200, 6300, 12300, 14400, 9000, 18600
        B = self.snap()
        VI = 12

        def vfence(e):
            t = fv.c
            fv.inc(e.engine_nop(), 1)
            e.wait_ge(fv.h, t + 1)

        def wd1_src(half, jm):
            t = K.wsp["wd1"][half].ap().tensor
            return AP(t, jm * 128, [[4096, 128], [128 * 4096, 8], [1, 128]])

        def h_sync3(e):
            e.wait_ge(vec.h, B["vec"])
            e.wait_ge(gp.h, B["gp"])
            for rr, off in ((R_D2B, 0), (R_LN2G, 1024), (R_LN2B, 2048)):
                ld.inc(e.dma_start(out=K.fa[:, RW + off:RW + off + 1024],
                                   in_=brow(K.rows_in.ap()[rr:rr + 1, :])), 16)
            for tc in range(16):
                if tc >= 1:
                    e.wait_ge(vec.h, B["vec"] + VI * tc)
                e.wait_ge(ld.h, B["ld"] + 16 * (3 + 5 * tc))
                ld.inc(e.dma_start(out=K.fa[:, HREF:HREF + 1024],
                                   in_=K.h2_hbm.ap()[tc]), 16)
                ld.inc(e.dma_start(out=K.fa[:, XT:XT + 1024],
                                   in_=K.x_in.ap()[HALO + tc * 128:
                                                   HALO + (tc + 1) * 128, :]), 16)
                ld.inc(e.dma_start(out=K.fa[:, GT:GT + 1024],
                                   in_=K.g_hbm.ap()[tc]), 16)
                par = tc % 2
                for half in range(2):
                    ld.inc(e.dma_start(
                        out=K.ba[:, 12288 + par * 2048 + half * 1024:
                                 12288 + par * 2048 + (half + 1) * 1024].rearrange(
                            "p (k t) -> p k t", k=8),
                        in_=AP(K.h2T_sp[half].ap().tensor, tc * 128,
                               [[TSL, 128], [128 * TSL, 8], [1, 128]])), 16)
                for jm in range(32):
                    jpar = jm % 2
                    sem = ld2 if jpar == 0 else ld3
                    snm = "ld2" if jpar == 0 else "ld3"
                    if jm >= 2:
                        e.wait_ge(te.h, B["te"] + 64 * tc + (jm - 1))
                    elif tc >= 1:
                        e.wait_ge(te.h, B["te"] + 64 * (tc - 1) + 31 + jm)
                    e.wait_ge(sem.h, B[snm] + 32 * (16 * tc + jm // 2))
                    for half in range(2):
                        sem.inc(e.dma_start(
                            out=K.ba[:, 8192 + jpar * 2048 + half * 1024:
                                     8192 + jpar * 2048 + (half + 1) * 1024].rearrange(
                                "p (k t) -> p k t", k=8),
                            in_=wd1_src(half, jm)), 16)


        def h_tensor(e):
            e.wait_ge(ld.h, B["ld"] + 16 * 3)
            for tc in range(16):
                par = tc % 2
                e.wait_ge(ld.h, B["ld"] + 16 * (3 + 5 * (tc + 1)))
                for jm in range(32):
                    jpar = jm % 2
                    e.wait_ge((ld2 if jpar == 0 else ld3).h,
                              B["ld2" if jpar == 0 else "ld3"]
                              + 32 * (16 * tc + jm // 2 + 1))
                    if jm >= 2:
                        e.wait_ge(sc.h, B["sc"] + 33 * tc + (jm - 1))
                    psd = K.ps_a[:, (jm % 2) * 512:(jm % 2) * 512 + 128]
                    wh = K.ba[:, 8192 + jpar * 2048:8192 + jpar * 2048 + 1024]
                    wl = K.ba[:, 8192 + jpar * 2048 + 1024:8192 + jpar * 2048 + 2048]
                    th = K.ba[:, 12288 + par * 2048:12288 + par * 2048 + 1024]
                    tl = K.ba[:, 12288 + par * 2048 + 1024:12288 + par * 2048 + 2048]
                    for kc in range(8):
                        e.ldweights(wh[:, kc * 128:(kc + 1) * 128])
                        e.matmul(out=psd, lhsT=wh[:, kc * 128:(kc + 1) * 128],
                                 rhs=th[:, kc * 128:(kc + 1) * 128],
                                 start=(kc == 0), stop=False)
                        e.matmul(out=psd, lhsT=wh[:, kc * 128:(kc + 1) * 128],
                                 rhs=tl[:, kc * 128:(kc + 1) * 128],
                                 start=False, stop=False)
                    for kc in range(8):
                        e.ldweights(wl[:, kc * 128:(kc + 1) * 128])
                        ins = e.matmul(out=psd, lhsT=wl[:, kc * 128:(kc + 1) * 128],
                                       rhs=th[:, kc * 128:(kc + 1) * 128],
                                       start=False, stop=(kc == 7))
                    te.inc(ins, 1)
                e.wait_ge(vec.h, B["vec"] + VI * tc + 8)
                for k in range(32):
                    kpar = k % 2
                    e.wait_ge(S["st" if kpar == 0 else "st3"].h,
                              B["st" if kpar == 0 else "st3"]
                              + 32 * (16 * tc + k // 2 + 1))
                    fh = K.ba[:, k * 128:(k + 1) * 128]
                    fl = K.ba[:, 4096 + k * 128:4096 + (k + 1) * 128]
                    woh = K.xnT_hs[:, kpar * 1024:(kpar + 1) * 1024]
                    wol = K.xnT_ls[:, kpar * 1024:(kpar + 1) * 1024]
                    e.ldweights(fh)
                    for nq in range(2):
                        e.matmul(out=K.ps_b[:, nq * 512:(nq + 1) * 512], lhsT=fh,
                                 rhs=woh[:, nq * 512:(nq + 1) * 512],
                                 start=(k == 0), stop=False)
                    for nq in range(2):
                        e.matmul(out=K.ps_b[:, nq * 512:(nq + 1) * 512], lhsT=fh,
                                 rhs=wol[:, nq * 512:(nq + 1) * 512],
                                 start=False, stop=False)
                    e.ldweights(fl)
                    for nq in range(2):
                        ins = e.matmul(out=K.ps_b[:, nq * 512:(nq + 1) * 512], lhsT=fl,
                                       rhs=woh[:, nq * 512:(nq + 1) * 512],
                                       start=False, stop=(k == 31))
                    te.inc(ins, 1)


        st2 = S["st"]

        def h_scalar(e):
            for tc in range(16):
                for jm in range(32):
                    if jm % 4 == 0 and jm > 0:
                        e.wait_ge(vec.h, B["vec"] + VI * tc + jm // 4)
                    e.wait_ge(te.h, B["te"] + 64 * tc + jm + 1)
                    sc.inc(e.activation(
                        out=K.fa[:, SCR2 + (jm % 4) * 128:SCR2 + (jm % 4) * 128 + 128],
                        in_=K.ps_a[:, (jm % 2) * 512:(jm % 2) * 512 + 128],
                        func=AF.Gelu,
                        bias=K.consts[:, K.CO_D1B + jm:K.CO_D1B + jm + 1]), 1)
                for k in range(32):
                    kpar = k % 2
                    semw = st2 if kpar == 0 else S["st3"]
                    swnm = "st" if kpar == 0 else "st3"
                    if k >= 2:
                        e.wait_ge(te.h, B["te"] + 64 * tc + 32 + (k - 1))
                    e.wait_ge(semw.h, B[swnm] + 32 * (16 * tc + k // 2))
                    for half in range(2):
                        src = K.wsp["wd2"][half].ap()[k * 128:(k + 1) * 128, :]
                        arena = K.xnT_hs if half == 0 else K.xnT_ls
                        semw.inc(e.dma_start(
                            out=arena[:, kpar * 1024:(kpar + 1) * 1024],
                            in_=src), 16)
                e.wait_ge(vec.h, B["vec"] + VI * tc + 10)
                sc.inc(e.activation(out=K.fa[:, CL2 + 3:CL2 + 4],
                                    in_=K.fa[:, CL2 + 2:CL2 + 3], func=AF.Sqrt,
                                    bias=K.consts[:, K.CC_EPSA:K.CC_EPSA + 1]), 1)

        def h_vector(e):
            c = K.fa
            for tc in range(16):
                for grp in range(8):
                    e.wait_ge(sc.h, B["sc"] + 33 * tc + 4 * (grp + 1))
                    src = c[:, SCR2:SCR2 + 512]
                    hdst = K.ba[:, grp * 512:(grp + 1) * 512]
                    ldst = K.ba[:, 4096 + grp * 512:4096 + (grp + 1) * 512]
                    e.tensor_copy(out=hdst, in_=src)
                    e.tensor_copy(out=c[:, SCR:SCR + 512], in_=hdst)
                    e.tensor_sub(out=c[:, SCR:SCR + 512], in0=src, in1=c[:, SCR:SCR + 512])
                    vec.inc(e.tensor_copy(out=ldst, in_=c[:, SCR:SCR + 512]), 1)
                e.wait_ge(te.h, B["te"] + 64 * tc + 64)
                e.tensor_copy(out=c[:, QP:QP + 1024], in_=K.ps_b[:, 0:1024])
                vec.inc(e.tensor_add(out=c[:, QP:QP + 1024], in0=c[:, QP:QP + 1024],
                                     in1=c[:, RW:RW + 1024]), 1)
                e.tensor_add(out=c[:, QP:QP + 1024], in0=c[:, QP:QP + 1024],
                             in1=c[:, HREF:HREF + 1024])
                e.tensor_reduce(out=c[:, CL2:CL2 + 1], in_=c[:, QP:QP + 1024],
                                axis=AX.X, op=ALU.add)
                e.tensor_mul(out=c[:, SCR:SCR + 1024], in0=c[:, QP:QP + 1024],
                             in1=c[:, QP:QP + 1024])
                e.tensor_reduce(out=c[:, CL2 + 1:CL2 + 2], in_=c[:, SCR:SCR + 1024],
                                axis=AX.X, op=ALU.add)
                vfence(e)
                e.tensor_scalar(out=c[:, CL2:CL2 + 2], in0=c[:, CL2:CL2 + 2],
                                scalar1=1.0 / DM, scalar2=None, op0=ALU.mult)
                vfence(e)
                e.tensor_mul(out=c[:, CL2 + 2:CL2 + 3], in0=c[:, CL2:CL2 + 1],
                             in1=c[:, CL2:CL2 + 1])
                vfence(e)
                vec.inc(e.tensor_sub(out=c[:, CL2 + 2:CL2 + 3],
                                     in0=c[:, CL2 + 1:CL2 + 2],
                                     in1=c[:, CL2 + 2:CL2 + 3]), 1)
                e.wait_ge(sc.h, B["sc"] + 33 * (tc + 1))
                e.reciprocal(out=c[:, CL2 + 4:CL2 + 5], in_=c[:, CL2 + 3:CL2 + 4])
                vfence(e)
                e.tensor_mul(out=c[:, CL2 + 5:CL2 + 6], in0=c[:, CL2:CL2 + 1],
                             in1=c[:, CL2 + 4:CL2 + 5])
                vfence(e)
                e.tensor_scalar(out=c[:, SCR:SCR + 1024], in0=c[:, QP:QP + 1024],
                                scalar1=c[:, CL2 + 4:CL2 + 5],
                                scalar2=c[:, CL2 + 5:CL2 + 6],
                                op0=ALU.mult, op1=ALU.subtract)
                e.tensor_mul(out=c[:, SCR:SCR + 1024], in0=c[:, SCR:SCR + 1024],
                             in1=c[:, RW + 1024:RW + 2048])
                e.tensor_add(out=c[:, SCR:SCR + 1024], in0=c[:, SCR:SCR + 1024],
                             in1=c[:, RW + 2048:RW + 3072])
                e.tensor_mul(out=c[:, SCR:SCR + 1024], in0=c[:, SCR:SCR + 1024],
                             in1=c[:, GT:GT + 1024])
                vec.inc(e.tensor_add(out=c[:, SCR:SCR + 1024], in0=c[:, SCR:SCR + 1024],
                                     in1=c[:, XT:XT + 1024]), 1)
                if tc >= 1:
                    e.wait_ge(gp.h, B["gp"] + 16 * tc)
                vec.inc(e.tensor_copy(out=c[:, 7400:7400 + 1024],
                                      in_=c[:, SCR:SCR + 1024]), 1)

        def h_gpsimd(e):
            fs = S["fs"]
            for tc in range(16):
                e.wait_ge(vec.h, B["vec"] + VI * (tc + 1))
                gp.inc(e.dma_start(out=K.out_t.ap()[tc * 128:(tc + 1) * 128, :],
                                   in_=K.fa[:, 7400:7400 + 1024]), 16)
                if "s" in K.dbg:
                    fs.inc(e.dma_start(out=K.dbg["s"].ap()[tc],
                                       in_=K.fa[:, 0:1024]), 16)
            if "s" in K.dbg:
                e.wait_ge(fs.h, fs.c)
            e.wait_ge(gp.h, gp.c)

        def h_final_waits(e):
            e.wait_ge(gp.h, self.S["gp"].c)

        K.blk.sync(h_sync3)
        K.blk.tensor(h_tensor)
        K.blk.scalar(h_scalar)
        K.blk.vector(h_vector)
        K.blk.gpsimd(h_gpsimd)

    def finish(self):
        self.es.close()
        return self.nc


def make_host_inputs(inputs, core):
    I = inputs
    g = lambda k: np.ascontiguousarray(I[k])
    x = g("L0_attn_input_tensor").reshape(16384, DM)
    t0 = core * TSL
    xc = np.zeros((TT, DM), np.float32)
    xc[HALO:] = x[t0:t0 + TSL]
    if core % 2 == 1:
        xc[:HALO] = x[t0 - HALO:t0]
    d = {}
    d["x"] = xc
    d["w_ip_T"] = np.ascontiguousarray(g("L0_attn_in_proj_w").T)
    d["w_xp_T"] = np.ascontiguousarray(g("L0_attn_x_proj_w").T)
    d["w_dt_T"] = np.ascontiguousarray(g("L0_attn_dt_proj_w").T)
    d["conv_wc"] = np.ascontiguousarray(g("L0_attn_conv_w").reshape(16, 128, 4))
    d["conv_bc"] = np.ascontiguousarray(g("L0_attn_conv_b").reshape(16, 128, 1))
    d["dt_bc"] = np.ascontiguousarray(g("L0_attn_dt_proj_b").reshape(16, 128, 1))
    d["a_log"] = np.ascontiguousarray(
        g("L0_attn_A_log")[core * DSL:(core + 1) * DSL].reshape(2, 128, NST))
    d["d_skip"] = np.ascontiguousarray(g("L0_attn_D_skip").reshape(16, 128, 1))
    d["w_op_T"] = np.ascontiguousarray(g("L0_attn_out_proj_w").T)
    d["w_q_T"] = np.ascontiguousarray(g("L0_attn_q_w").T)
    d["w_k_T"] = np.ascontiguousarray(g("L0_attn_k_w").T)
    d["w_p_T"] = np.ascontiguousarray(g("L0_attn_proj_w").T)
    d["w_f_T"] = np.ascontiguousarray(g("L0_attn_fl_w").T)
    d["w_d1_T"] = np.ascontiguousarray(g("L0_attn_d1_w").T)
    d["w_d2_T"] = np.ascontiguousarray(g("L0_attn_d2_w").T)
    d["d1_bc"] = np.ascontiguousarray(g("L0_attn_d1_b").reshape(32, 128, 1))
    d["w_fi"] = g("L0_ffn_W_in")
    d["w_fi_T"] = np.ascontiguousarray(g("L0_ffn_W_in").T)
    d["fi_bc"] = np.ascontiguousarray(g("L0_ffn_b_in").reshape(32, 128, 1))
    d["w_fo"] = g("L0_ffn_W_out")
    rows = np.zeros((16, DM), np.float32)
    rows[R_QB] = g("L0_attn_q_b"); rows[R_KB] = g("L0_attn_k_b")
    rows[R_WG] = g("L0_attn_w_g").reshape(-1)
    rows[R_PB] = g("L0_attn_proj_b"); rows[R_FB] = g("L0_attn_fl_b")
    rows[R_D2B] = g("L0_attn_d2_b"); rows[R_FOB] = g("L0_ffn_b_out")
    rows[R_LN0G] = g("L0_attn_ln0_g"); rows[R_LN0B] = g("L0_attn_ln0_b")
    rows[R_LNIG] = g("L0_attn_lni_g"); rows[R_LNIB] = g("L0_attn_lni_b")
    rows[R_LN1G] = g("L0_attn_ln1_g"); rows[R_LN1B] = g("L0_attn_ln1_b")
    rows[R_LN2G] = g("L0_attn_ln2_g"); rows[R_LN2B] = g("L0_attn_ln2_b")
    d["rows_in"] = rows
    oh = np.zeros((1, 4), np.float32); oh[0, core // 2] = 1.0
    d["onehot_r"] = oh
    d["onehot_c"] = np.ascontiguousarray(oh.reshape(4, 1))
    d["ident"] = np.eye(128, dtype=np.float32)
    return d


_BUILT = None


def build():
    global _BUILT
    kb = KB()
    kb.ph_setup()
    if PHASE_LIMIT >= 1:
        kb.ph_p0()
    if PHASE_LIMIT >= 2:
        kb.ph_pw()
    if PHASE_LIMIT >= 3:
        kb.ph_p1()
    if PHASE_LIMIT >= 4:
        kb.ph_p2()
    if PHASE_LIMIT >= 5:
        kb.ph_ffn()
    if PHASE_LIMIT >= 6:
        kb.ph_p3a()
    if PHASE_LIMIT >= 7:
        kb.ph_p3b()
    if PHASE_LIMIT >= 8:
        kb.ph_p3c()
    if PHASE_LIMIT >= 9:
        kb.ph_p3d()
    if PHASE_LIMIT >= 10:
        kb.ph_p3f1()
        if SUB10 >= 2:
            kb.ph_mm_ln("wp", kb.gkT_sp, R_PB, R_LNIG, R_LNIB, "mul", kb.hlT_sp,
                        store_h2=(SUB10 == 2))
        if SUB10 >= 3:
            kb.ph_mm_ln("wf", kb.hlT_sp, R_FB, R_LN1G, R_LN1B, "add", kb.h2T_sp,
                        store_h2=True)
    if PHASE_LIMIT >= 11:
        kb.ph_p3h()
    return kb.finish()


def kernel(**inputs):
    nc = build()
    in_maps = [make_host_inputs(inputs, c) for c in range(NCORE)]
    with allow_ici_all_to_all():
        res = run_bass_kernel_spmd(nc, in_maps, core_ids=list(range(NCORE)))
    outs = [res.results[c].get("out") for c in range(NCORE)]
    full = np.concatenate([o.reshape(TSL, DM) for o in outs], 0)
    kernel._last_results = res
    return full.reshape(4, 4096, DM)
